# version 02
# baseline (speedup 1.0000x reference)
import numpy as np
import ml_dtypes
import concourse.bass as bass
import concourse.mybir as mybir
from concourse.bass_utils import run_bass_kernel_spmd

F32 = mybir.dt.float32
BF16 = mybir.dt.bfloat16
ALU = mybir.AluOpType
AF = mybir.ActivationFunctionType
AX = mybir.AxisListType

B, L, D, H = 4, 4096, 1024, 16
DH = 64
HG = 8
NJT = 32
WINDOW = 256
NLAYER = 6
DIM_FF2 = 5460
DIM_INNER = 2730
QK_SCALE = 8.0

TC = 1024
NCH = L // TC
NT = TC // 128
NS = TC // 512
NMT = 3 * HG * DH // 128
NHP = HG // 2

T2 = 2048
HALO = 256
NTH = HALO // 128
NTL = NTH + 16
NTO = 16
LT = HALO + T2
NXB = 3


class Sem:

    def __init__(self, h):
        self.h = h
        self.val = 0

    def inc(self, inst, n=1):
        inst.then_inc(self.h, n)
        self.val += n
        return self.val


def build_fused(nlayer=NLAYER, stop_after=None, dbg=None):
    nc = bass.Bass("TRN2", target_bir_lowering=False)

    x_h = nc.dram_tensor("mx_x", [L, D], F32, kind="ExternalInput").ap()
    wqkvT_h = nc.dram_tensor("mx_wqkvT", [D, 3 * HG * DH], F32,
                             kind="ExternalInput").ap()
    bqkv_h = nc.dram_tensor("mx_bqkv", [128, NMT], F32,
                            kind="ExternalInput").ap()
    lam_h = nc.dram_tensor("mx_lam", [HG, 128, NJT], F32,
                           kind="ExternalInput").ap()
    gam_h = nc.dram_tensor("mx_gam", [HG, 128, NJT], F32,
                           kind="ExternalInput").ap()
    wstT_h = nc.dram_tensor("mx_wstT", [HG, DH, DH], F32,
                            kind="ExternalInput").ap()
    selw_h = nc.dram_tensor("mx_selw", [128, 128], BF16,
                            kind="ExternalInput").ap()

    x_hh = nc.dram_tensor("r_x", [T2, D], F32, kind="ExternalInput").ap()
    outwT_h = nc.dram_tensor("r_outwT", [D, D], F32, kind="ExternalInput").ap()
    outb_h = nc.dram_tensor("r_outb", [1, D], F32, kind="ExternalInput").ap()
    fwqkv_h = nc.dram_tensor("r_fwqkv", [NLAYER, D, 4, 768], F32,
                             kind="ExternalInput").ap()
    fwo_h = nc.dram_tensor("r_fwo", [NLAYER, D, D], F32, kind="ExternalInput").ap()
    fw1_h = nc.dram_tensor("r_fw1", [NLAYER, D, DIM_FF2], F32,
                           kind="ExternalInput").ap()
    fw2_h = nc.dram_tensor("r_fw2", [NLAYER, DIM_INNER, D], F32,
                           kind="ExternalInput").ap()
    fb1_h = nc.dram_tensor("r_fb1", [NLAYER, 128, 44], F32,
                           kind="ExternalInput").ap()
    fb2_h = nc.dram_tensor("r_fb2", [NLAYER, 1, D], F32, kind="ExternalInput").ap()
    flng_h = nc.dram_tensor("r_flng", [NLAYER, 128, 8], F32,
                            kind="ExternalInput").ap()
    flnb_h = nc.dram_tensor("r_flnb", [NLAYER, 128, 8], F32,
                            kind="ExternalInput").ap()
    fqs_h = nc.dram_tensor("r_fqs", [NLAYER, 1, DH], F32, kind="ExternalInput").ap()
    fks_h = nc.dram_tensor("r_fks", [NLAYER, 1, DH], F32, kind="ExternalInput").ap()
    tabs_h = nc.dram_tensor("r_tabs", [4, LT, DH], F32, kind="ExternalInput").ap()
    mask_h = nc.dram_tensor("r_mask", [2, 2, 128, 512], BF16,
                            kind="ExternalInput").ap()
    id_h = nc.dram_tensor("r_id128", [128, 128], BF16, kind="ExternalInput").ap()
    ones_h = nc.dram_tensor("r_ones", [1, 128], BF16, kind="ExternalInput").ap()
    hgate_h = nc.dram_tensor("r_hgate", [128, 2], F32, kind="ExternalInput").ap()
    out_h = nc.dram_tensor("r_out", [T2, D], F32, kind="ExternalOutput").ap()
    dbg_h = nc.dram_tensor("r_dbg", [128, 3 * 4608], BF16, kind="ExternalOutput").ap()

    y2in = nc.dram_tensor("y2in", [4 * 128, L], BF16)
    y2out = nc.dram_tensor("y2out", [8 * 128, L], BF16)
    xmid_st = nc.dram_tensor("xmid_stash", [T2, D], F32)
    hx_in = nc.dram_tensor("hx_in", [NXB * 128, D], BF16)
    hx_out = nc.dram_tensor("hx_out", [2 * NXB * 128, D], BF16)

    from contextlib import ExitStack
    es = ExitStack()
    s_ld_h = es.enter_context(nc.semaphore("s_ld"))
    s_lg_h = es.enter_context(nc.semaphore("s_lg"))
    s_mm_h = es.enter_context(nc.semaphore("s_mm"))
    s_ve_h = es.enter_context(nc.semaphore("s_ve"))
    s_sc_h = es.enter_context(nc.semaphore("s_sc"))
    s_tr_h = es.enter_context(nc.semaphore("s_tr"))
    s_st_h = es.enter_context(nc.semaphore("s_st"))
    s_cc_h = es.enter_context(nc.semaphore("s_cc"))
    s_ld, s_lg, s_mm = Sem(s_ld_h), Sem(s_lg_h), Sem(s_mm_h)
    s_ve, s_sc, s_tr = Sem(s_ve_h), Sem(s_sc_h), Sem(s_tr_h)
    s_st, s_cc = Sem(s_st_h), Sem(s_cc_h)

    es_mix = ExitStack()
    xl = es_mix.enter_context(nc.sbuf_tensor("m_xl", [128, NT * D], BF16))
    mxdm = es_mix.enter_context(nc.sbuf_tensor("m_xdm", [128, 8 * TC], BF16))
    wq = es_mix.enter_context(nc.sbuf_tensor("m_wq", [128, 8 * 3 * HG * DH], BF16))
    bq = es_mix.enter_context(nc.sbuf_tensor("m_bq", [128, NMT], F32))
    qkv = es_mix.enter_context(nc.sbuf_tensor("m_qkv", [128, NMT * TC], BF16))
    q2 = es_mix.enter_context(nc.sbuf_tensor("m_q2", [128, 2 * TC], BF16))
    k2 = es_mix.enter_context(nc.sbuf_tensor("m_k2", [128, 2 * TC], BF16))
    vrep = es_mix.enter_context(nc.sbuf_tensor("m_vrep", [128, 2 * 2 * TC], BF16))
    ubuf = es_mix.enter_context(nc.sbuf_tensor("m_ubuf", [128, 2 * TC], BF16))
    straj = es_mix.enter_context(nc.sbuf_tensor("m_straj", [128, 2 * TC], BF16))
    atl = es_mix.enter_context(nc.sbuf_tensor("m_atl", [128, 2 * 2 * TC], BF16))
    rcol = es_mix.enter_context(nc.sbuf_tensor("m_rcol", [128, HG * NJT], F32))
    gcol = es_mix.enter_context(nc.sbuf_tensor("m_gcol", [128, HG * NJT], F32))
    scar = es_mix.enter_context(nc.sbuf_tensor("m_scar", [128, HG * NJT], F32))
    wst = es_mix.enter_context(nc.sbuf_tensor("m_wst", [128, NHP * 128], BF16))
    selw = es_mix.enter_context(nc.sbuf_tensor("m_selw", [128, 128], BF16))
    ydm = es_mix.enter_context(nc.sbuf_tensor("m_ydm", [128, NHP * L], BF16))
    ps_qkv = es_mix.enter_context(nc.psum_tensor("m_ps_qkv", [128, 2 * 512], F32))
    ps_y = es_mix.enter_context(nc.psum_tensor("m_ps_y", [128, 2 * 512], F32))

    block = es.enter_context(nc.Block())
    es_rest = ExitStack()

    @block.sync
    def _(sync):
        v = nc.vector
        sc = nc.scalar
        te = nc.tensor
        g = nc.gpsimd

        def barrier():
            for eng in (v, sc, te, g, sync):
                eng.wait_ge(s_ld_h, s_ld.val)
                eng.wait_ge(s_lg_h, s_lg.val)
                eng.wait_ge(s_tr_h, s_tr.val)
                eng.wait_ge(s_st_h, s_st.val)
                eng.wait_ge(s_mm_h, s_mm.val)
                eng.wait_ge(s_ve_h, s_ve.val)
                eng.wait_ge(s_sc_h, s_sc.val)
                eng.wait_ge(s_cc_h, s_cc.val)

        def vwait():
            v.wait_ge(s_ve_h, s_ve.val)

        for kt in range(8):
            s_lg.inc(g.dma_start(out=wq[:, kt * 3 * HG * DH:(kt + 1) * 3 * HG * DH],
                in_=wqkvT_h[kt * 128:(kt + 1) * 128, :]), 16)
        s_ld.inc(sync.dma_start(out=bq[:, :], in_=bqkv_h[:, :]), 16)
        for h in range(HG):
            s_ld.inc(sync.dma_start(out=rcol[:, h * NJT:(h + 1) * NJT],
                                    in_=lam_h[h]), 16)
            s_ld.inc(sync.dma_start(out=gcol[:, h * NJT:(h + 1) * NJT],
                                    in_=gam_h[h]), 16)
        s_ve.inc(v.memset(wst[:, :], 0.0))
        s_ld.inc(sync.dma_start(out=selw[:, :], in_=selw_h[:, :]), 16)
        g.wait_ge(s_ve_h, s_ve.val)
        for hp in range(NHP):
            s_lg.inc(g.dma_start(out=wst[0:64, hp * 128:hp * 128 + 64],
                                 in_=wstT_h[2 * hp]), 16)
            s_lg.inc(g.dma_start(out=wst[64:128, hp * 128 + 64:hp * 128 + 128],
                                 in_=wstT_h[2 * hp + 1]), 16)

        sc.wait_ge(s_ld_h, s_ld.val)
        v.wait_ge(s_ld_h, s_ld.val)
        sc.wait_ge(s_lg_h, s_lg.val)
        v.wait_ge(s_lg_h, s_lg.val)
        s_sc.inc(sc.activation(out=rcol[:, :], in_=rcol[:, :],
                               func=AF.Sigmoid, bias=0.0, scale=-1.0))
        s_ve.inc(v.tensor_scalar_mul(out=gcol[:, :], in0=gcol[:, :],
                                     scalar1=0.125))
        s_ve.inc(v.memset(scar[:, :], 0.0))
        barrier()

        def dmrow(c, h):
            gch = c * HG * DH + h * DH
            return (gch // 128, gch % 128)

        for ch in range(NCH):
            t0 = ch * TC
            for tt in range(NT):
                s_lg.inc(g.dma_start(out=xl[:, tt * D:(tt + 1) * D],
                    in_=x_h[t0 + tt * 128: t0 + (tt + 1) * 128, :]), 16)
            sync.wait_ge(s_lg_h, s_lg.val)
            for tt in range(NT):
                for dm in range(8):
                    s_tr.inc(sync.dma_start(out=mxdm[:, dm * TC + tt * 128: dm * TC + (tt + 1) * 128],
                        in_=xl[:, tt * D + dm * 128: tt * D + (dm + 1) * 128],
                        transpose=True), 16)
            te.wait_ge(s_tr_h, s_tr.val)

            evacs = []
            for m in range(NMT):
                for ns in range(NS):
                    qidx = m * NS + ns
                    bank = qidx % 2
                    if qidx >= 2:
                        te.wait_ge(s_sc_h, evacs[qidx - 2])
                    for kt in range(8):
                        mm = te.matmul(
                            ps_qkv[:, bank * 512:(bank + 1) * 512],
                            wq[:, kt * 3 * HG * DH + m * 128:
                               kt * 3 * HG * DH + (m + 1) * 128],
                            mxdm[:, kt * TC + ns * 512: kt * TC + (ns + 1) * 512],
                            start=(kt == 0), stop=(kt == 7))
                    s_mm.inc(mm)
                    sc.wait_ge(s_mm_h, s_mm.val)
                    s_sc.inc(sc.activation(
                        out=qkv[:, m * TC + ns * 512: m * TC + (ns + 1) * 512],
                        in_=ps_qkv[:, bank * 512:(bank + 1) * 512],
                        func=AF.Identity, bias=bq[:, m:m + 1], scale=1.0))
                    evacs.append(s_sc.val)
            barrier()

            for hp in range(NHP):
                hA, hB = 2 * hp, 2 * hp + 1
                for (hh, dst0) in ((hA, 0), (hB, TC)):
                    mq, rq = dmrow(0, hh)
                    mk, rk = dmrow(1, hh)
                    for half in range(2):
                        s_lg.inc(g.dma_start(out=q2[64 * half:64 * half + 64, dst0:dst0 + TC],
                            in_=qkv[rq:rq + 64, mq * TC: mq * TC + TC]), 16)
                        s_lg.inc(g.dma_start(out=k2[64 * half:64 * half + 64, dst0:dst0 + TC],
                            in_=qkv[rk:rk + 64, mk * TC: mk * TC + TC]), 16)

                mqA, rqA = dmrow(0, hA)
                for ns in range(NS):
                    bank = ns % 2
                    te.matmul(
                        ps_y[:, bank * 512:(bank + 1) * 512],
                        wst[:, hp * 128:(hp + 1) * 128],
                        qkv[:, mqA * TC + ns * 512: mqA * TC + (ns + 1) * 512],
                        start=True, stop=False, skip_group_check=True)

                jt_mm = []
                jt_ve = []
                for jt in range(NJT):
                    par = jt % 2
                    if jt >= 1:
                        g.wait_ge(s_ve_h, jt_ve[jt - 1])
                    for (hh, off) in ((hA, 0), (hB, TC)):
                        mv, rv = dmrow(2, hh)
                        src = qkv[rv + 2 * jt: rv + 2 * jt + 2,
                                  mv * TC: mv * TC + TC]
                        s_lg.inc(g.dma_start(out=vrep[:, par * 2 * TC + off:
                                     par * 2 * TC + off + TC],
                            in_=src.unsqueeze(1).broadcast_to([2, 64, TC])), 16)
                    v.wait_ge(s_lg_h, s_lg.val)
                    if jt >= 2:
                        v.wait_ge(s_mm_h, jt_mm[jt - 2])
                    for (hh, off) in ((hA, 0), (hB, TC)):
                        hcol = hh * NJT + jt
                        v.wait_ge(s_ve_h, s_ve.val)
                        s_ve.inc(v.scalar_tensor_tensor(
                            out=ubuf[:, off:off + TC],
                            in0=k2[:, off:off + TC],
                            scalar=gcol[:, hcol:hcol + 1],
                            in1=vrep[:, par * 2 * TC + off:
                                     par * 2 * TC + off + TC],
                            op0=ALU.mult, op1=ALU.mult))
                        v.wait_ge(s_ve_h, s_ve.val)
                        s_ve.inc(v.tensor_tensor_scan(
                            out=straj[:, off:off + TC],
                            data0=rcol[:, hcol:hcol + 1].broadcast_to([128, TC]),
                            data1=ubuf[:, off:off + TC],
                            initial=scar[:, hcol:hcol + 1],
                            op0=ALU.mult, op1=ALU.add))
                        v.wait_ge(s_ve_h, s_ve.val)
                        s_ve.inc(v.tensor_copy(scar[:, hcol:hcol + 1],
                                               straj[:, off + TC - 1:off + TC]))
                        v.wait_ge(s_ve_h, s_ve.val)
                        s_ve.inc(v.tensor_tensor(
                            out=atl[:, par * 2 * TC + off:
                                    par * 2 * TC + off + TC],
                            in0=straj[:, off:off + TC],
                            in1=q2[:, off:off + TC], op=ALU.mult))
                    jt_ve.append(s_ve.val)
                    te.wait_ge(s_ve_h, s_ve.val)
                    last = jt == NJT - 1
                    for ns in range(NS):
                        bank = ns % 2
                        for (hh, off, pbase) in ((hA, 0, 0), (hB, TC, 64)):
                            mm = te.matmul(
                                ps_y[pbase:pbase + 64,
                                     bank * 512:(bank + 1) * 512],
                                selw[:, 64 - 2 * jt:128 - 2 * jt],
                                atl[:, par * 2 * TC + off + ns * 512:
                                    par * 2 * TC + off + (ns + 1) * 512],
                                start=False, stop=last,
                                skip_group_check=True)
                    s_mm.inc(mm)
                    jt_mm.append(s_mm.val)

                for ns in range(NS):
                    bank = ns % 2
                    sc.wait_ge(s_mm_h, s_mm.val)
                    s_sc.inc(sc.activation(
                        out=ydm[:, hp * L + t0 + ns * 512:
                                hp * L + t0 + (ns + 1) * 512],
                        in_=ps_y[:, bank * 512:(bank + 1) * 512],
                        func=AF.Identity, bias=0.0, scale=1.0))
                barrier()

        sync.wait_ge(s_sc_h, s_sc.val)
        for hp in range(NHP):
            s_st.inc(sync.dma_start(out=y2in.ap()[hp * 128:(hp + 1) * 128, :],
                in_=ydm[:, hp * L:(hp + 1) * L]), 16)
        g.wait_ge(s_st_h, s_st.val)
        s_cc.inc(g.collective_compute(
            "AllGather", ALU.bypass,
            replica_groups=[[0, 1], [2, 3], [4, 5], [6, 7]],
            ins=[y2in.ap().opt()], outs=[y2out.ap().opt()]), 1)
        barrier()

        es_mix.close()
        xown = es_rest.enter_context(nc.sbuf_tensor("xown", [128, NTO * D], F32))
        xhalo = es_rest.enter_context(nc.sbuf_tensor("xhalo", [128, NTH * D], BF16))
        xprev = es_rest.enter_context(nc.sbuf_tensor("xprev", [1, D], BF16))
        XST = 16 + LT
        xdm = es_rest.enter_context(nc.sbuf_tensor("xdm", [128, 8 * XST], BF16))
        att = es_rest.enter_context(nc.sbuf_tensor("att", [128, 4 * T2], BF16))
        tabc = es_rest.enter_context(nc.sbuf_tensor("tabc", [128, 4 * NTL * DH], BF16))
        masks = es_rest.enter_context(nc.sbuf_tensor("masks", [128, 4 * 512], BF16))
        id128 = es_rest.enter_context(nc.sbuf_tensor("id128", [128, 128], BF16))
        ones1 = es_rest.enter_context(nc.sbuf_tensor("ones1", [1, 128], BF16))
        onesc = es_rest.enter_context(nc.sbuf_tensor("onesc", [128, 1], BF16))
        hgate = es_rest.enter_context(nc.sbuf_tensor("hgate", [128, 2], F32))
        arena = es_rest.enter_context(nc.sbuf_tensor("arena", [128, 30720], BF16))
        qL = arena[:, 0:4608]
        kL = arena[:, 4608:9216]
        vL = arena[:, 9216:13824]
        wqs = arena[:, 13824:19968]
        qdm = arena[:, 13824:18432]
        kdm = arena[:, 0:4608]
        ffd = arena[:, 0:11264]
        w2s = arena[:, 11264:22528]
        w1s = arena[:, 22528:24576]
        rrep = arena[:, 24576:26624].bitcast(F32)
        xnb = arena[:, 26624:30720]
        hxs = arena[:, 0:NXB * D]
        wos = arena[:, 0:4096]
        yscr = arena[:, 4096:12288]
        tbsc = arena[:, 4096:5248]
        srow = es_rest.enter_context(nc.sbuf_tensor("srow", [64, 512], F32))
        gcolb = es_rest.enter_context(nc.sbuf_tensor("gcolb", [128, 8], F32))
        bcolb = es_rest.enter_context(nc.sbuf_tensor("bcolb", [128, 8], F32))
        b1c = es_rest.enter_context(nc.sbuf_tensor("b1c", [128, 44], F32))
        brow = es_rest.enter_context(nc.sbuf_tensor("brow", [1, 1024], BF16))
        qsr = es_rest.enter_context(nc.sbuf_tensor("qsr", [128, 2 * DH], F32))
        psc = es_rest.enter_context(nc.sbuf_tensor("psc", [128, 1024], F32))
        pcol = es_rest.enter_context(nc.sbuf_tensor("pcol", [128, 32], F32))
        ln_sc = psc[:, 0:512].bitcast(BF16)
        sq_sc = ln_sc[:, 0:512]
        nmx = pcol[:, 16:18]
        den = pcol[:, 18:20]
        rde = pcol[:, 20:22]
        abuf = psc[:, 0:512].bitcast(BF16)
        atd = psc[:, 512:1024].bitcast(BF16)
        ps_a = es_rest.enter_context(nc.psum_tensor("ps_a", [128, 2 * 512], F32))
        ps_b = es_rest.enter_context(nc.psum_tensor("ps_b", [128, 2 * 512], F32))
        ps_av = es_rest.enter_context(nc.psum_tensor("ps_av", [128, 2048], F32))

        sync.wait_ge(s_ld_h, s_ld.val)
        s_ld.inc(sync.dma_start(out=id128[:], in_=id_h[:, :]), 16)
        sync.wait_ge(s_ld_h, s_ld.val)
        s_ld.inc(sync.dma_start(out=ones1[:], in_=ones_h[:, :]), 16)
        sync.wait_ge(s_ld_h, s_ld.val)
        s_ld.inc(sync.dma_start(out=hgate[:], in_=hgate_h[:, :]), 16)
        for mi in range(4):
            sync.wait_ge(s_ld_h, s_ld.val)
            s_ld.inc(sync.dma_start(out=masks[:, mi * 512:(mi + 1) * 512],
                in_=mask_h[mi // 2, mi % 2]), 16)
        s_ve.inc(v.memset(onesc[:], 1.0))
        s_ve.inc(v.memset(pcol[:, 22:23], 1e-5))
        s_ve.inc(v.memset(xhalo[:, :], 0.0))
        s_ve.inc(v.memset(xprev[:, :], 0.0))
        for tt in range(NTO):
            sync.wait_ge(s_ld_h, s_ld.val)
            s_ld.inc(sync.dma_start(out=xown[:, tt * D:(tt + 1) * D],
                                    in_=x_hh[tt * 128:(tt + 1) * 128, :]), 16)
        g.wait_ge(s_lg_h, s_lg.val)
        s_lg.inc(g.dma_start(out=brow[:, :], in_=outb_h[:, :]), 16)
        barrier()

        def load_wproj_wo(li, half):
            for kp in range(4):
                g.wait_ge(s_lg_h, s_lg.val)
                s_lg.inc(g.dma_start(out=wos[:, kp * 1024:(kp + 1) * 1024],
                    in_=fwo_h[li, (half * 4 + kp) * 128:
                              (half * 4 + kp + 1) * 128, :]), 16)

        def proj_half(with_bias):
            for tt in range(NTO):
                for oc in range(2):
                    idx = tt * 2 + oc
                    bank = idx % 2
                    if idx >= 2:
                        te.wait_ge(s_ve_h, s_ve.val - 1)
                    for kp in range(4):
                        mm = te.matmul(
                            ps_a[:, bank * 512:(bank + 1) * 512],
                            att[:, kp * T2 + tt * 128:
                                kp * T2 + (tt + 1) * 128],
                            wos[:, kp * 1024 + oc * 512:
                                kp * 1024 + (oc + 1) * 512],
                            start=(kp == 0),
                            stop=(kp == 3 and not with_bias))
                    if with_bias:
                        mm = te.matmul(
                            ps_a[:, bank * 512:(bank + 1) * 512],
                            ones1[:, :], brow[:, oc * 512:(oc + 1) * 512],
                            start=False, stop=True)
                    s_mm.inc(mm)
                    v.wait_ge(s_mm_h, s_mm.val)
                    vwait()
                    s_ve.inc(v.tensor_tensor(
                        out=xown[:, tt * D + oc * 512: tt * D + (oc + 1) * 512],
                        in0=xown[:, tt * D + oc * 512: tt * D + (oc + 1) * 512],
                        in1=ps_a[:, bank * 512:(bank + 1) * 512], op=ALU.add))

        def ln_inplace():
            for tt in range(NTO):
                xt = xown[:, tt * D:(tt + 1) * D]
                vwait()
                s_ve.inc(v.tensor_reduce(out=pcol[:, 0:1], in_=xt,
                                         axis=AX.X, op=ALU.add))
                vwait()
                s_ve.inc(v.tensor_scalar_mul(out=pcol[:, 1:2],
                                             in0=pcol[:, 0:1], scalar1=1.0 / D))
                vwait()
                s_ve.inc(v.scalar_tensor_tensor(
                    out=ln_sc[:, :], in0=xt, scalar=pcol[:, 1:2], in1=xt,
                    op0=ALU.subtract, op1=ALU.mult, accum_out=pcol[:, 2:3]))
                sc.wait_ge(s_ve_h, s_ve.val)
                s_sc.inc(sc.activation(out=pcol[:, 3:4], in_=pcol[:, 2:3],
                                       func=AF.Sqrt, bias=pcol[:, 22:23],
                                       scale=1.0 / D))
                v.wait_ge(s_sc_h, s_sc.val)
                s_ve.inc(v.reciprocal(out=pcol[:, 4:5], in_=pcol[:, 3:4]))
                vwait()
                s_ve.inc(v.scalar_tensor_tensor(
                    out=xt, in0=xt, scalar=pcol[:, 1:2],
                    in1=pcol[:, 4:5].broadcast_to([128, D]),
                    op0=ALU.subtract, op1=ALU.mult))

        for half in range(2):
            blend_done = {}
            for kp in range(4):
                par = kp % 2
                scr = yscr[:, par * 4096:(par + 1) * 4096]
                if kp >= 2:
                    g.wait_ge(s_ve_h, blend_done[kp - 2])
                g.wait_ge(s_lg_h, s_lg.val)
                s_lg.inc(g.dma_start(out=scr,
                    in_=y2out.ap()[(half * 4 + kp) * 128:
                                   (half * 4 + kp + 1) * 128, :]), 16)
                v.wait_ge(s_lg_h, s_lg.val)
                att_t = att[:, kp * T2:(kp + 1) * T2]
                vwait()
                s_ve.inc(v.tensor_scalar_mul(out=att_t, in0=scr[:, 0:T2],
                                             scalar1=hgate[:, 1:2]))
                vwait()
                s_ve.inc(v.scalar_tensor_tensor(
                    out=att_t, in0=scr[:, T2:2 * T2], scalar=hgate[:, 0:1],
                    in1=att_t, op0=ALU.mult, op1=ALU.add))
                blend_done[kp] = s_ve.val
                g.wait_ge(s_lg_h, s_lg.val)
                s_lg.inc(g.dma_start(out=wos[:, kp * 1024:(kp + 1) * 1024],
                    in_=outwT_h[(half * 4 + kp) * 128:
                                (half * 4 + kp + 1) * 128, :]), 16)
            barrier()
            if dbg == "att0" and half == 0:
                sync.wait_ge(s_st_h, s_st.val)
                s_st.inc(sync.dma_start(out=dbg_h[:, 0:4 * T2],
                                        in_=att[:, :]), 16)
                barrier()
            proj_half(with_bias=(half == 1))
            barrier()
        ln_inplace()
        barrier()
        for tt in range(NTO):
            sync.wait_ge(s_st_h, s_st.val)
            s_st.inc(sync.dma_start(out=xmid_st.ap()[tt * 128:(tt + 1) * 128, :],
                                    in_=xown[:, tt * D:(tt + 1) * D]), 16)
        barrier()

        def halo_exchange():
            for i, tt in enumerate(range(13, 16)):
                s_sc.inc(sc.copy(hxs[:, i * D:(i + 1) * D],
                                 xown[:, tt * D:(tt + 1) * D]))
            sync.wait_ge(s_sc_h, s_sc.val)
            for i in range(NXB):
                sync.wait_ge(s_st_h, s_st.val)
                s_st.inc(sync.dma_start(out=hx_in.ap()[i * 128:(i + 1) * 128, :],
                    in_=hxs[:, i * D:(i + 1) * D]), 16)
            g.wait_ge(s_st_h, s_st.val)
            s_cc.inc(g.collective_compute(
                "AllGather", ALU.bypass,
                replica_groups=[[0, 1], [2, 3], [4, 5], [6, 7]],
                ins=[hx_in.ap().opt()], outs=[hx_out.ap().opt()]), 1)
            sync.wait_ge(s_cc_h, s_cc.val)
            for i in range(NTH):
                sync.wait_ge(s_ld_h, s_ld.val)
                s_ld.inc(sync.dma_start(out=xhalo[:, i * D:(i + 1) * D],
                    in_=hx_out.ap()[128 + i * 128: 128 + (i + 1) * 128, :]), 16)
            sync.wait_ge(s_ld_h, s_ld.val)
            s_ld.inc(sync.dma_start(out=xprev[:, :],
                                    in_=hx_out.ap()[127:128, :]), 16)
            v.wait_ge(s_ld_h, s_ld.val)
            for i in range(NTH):
                vwait()
                s_ve.inc(v.tensor_scalar_mul(out=xhalo[:, i * D:(i + 1) * D],
                                             in0=xhalo[:, i * D:(i + 1) * D],
                                             scalar1=hgate[:, 0:1]))
            vwait()
            s_ve.inc(v.tensor_scalar_mul(out=xprev[:, :], in0=xprev[:, :],
                                         scalar1=hgate[0:1, 0:1]))
            barrier()

        def build_xdm():
            s_ve.inc(v.memset(xdm[:, :], 0.0))
            barrier()
            for tt in range(NTL):
                if tt < NTH:
                    src = xhalo[:, tt * D:(tt + 1) * D]
                else:
                    sc.wait_ge(s_tr_h, s_tr.val)
                    s_sc.inc(sc.copy(ln_sc[:, :],
                                     xown[:, (tt - NTH) * D:(tt - NTH + 1) * D]))
                    sync.wait_ge(s_sc_h, s_sc.val)
                    src = ln_sc[:, :]
                for dm in range(8):
                    sync.wait_ge(s_tr_h, s_tr.val)
                    s_tr.inc(sync.dma_start(out=xdm[:, dm * XST + 16 + tt * 128:
                                dm * XST + 16 + (tt + 1) * 128],
                        in_=src[:, dm * 128:(dm + 1) * 128],
                        transpose=True), 16)
            for dm in range(4, 8):
                sync.wait_ge(s_tr_h, s_tr.val)
                sync.wait_ge(s_ld_h, s_ld.val)
                s_ld.inc(sync.dma_start(out=xdm[:, dm * XST + 15: dm * XST + 16],
                    in_=xprev[:, dm * 128:(dm + 1) * 128]), 16)
            barrier()

        def xs_slice(dm, c0, c1):
            off = dm * XST + (16 if dm < 4 else 15)
            return xdm[:, off + c0: off + c1]

        def emit_layer(li):
            g.wait_ge(s_lg_h, s_lg.val)
            s_lg.inc(g.dma_start(out=qsr[:, 0:DH],
                                 in_=fqs_h[li].broadcast_to([128, DH])), 16)
            g.wait_ge(s_lg_h, s_lg.val)
            s_lg.inc(g.dma_start(out=qsr[:, DH:2 * DH],
                                 in_=fks_h[li].broadcast_to([128, DH])), 16)
            v.wait_ge(s_lg_h, s_lg.val)
            vwait()
            s_ve.inc(v.tensor_scalar_mul(out=qsr[:, 0:DH], in0=qsr[:, 0:DH],
                                         scalar1=QK_SCALE))
            for ti in range(4):
                g.wait_ge(s_lg_h, s_lg.val)
                s_lg.inc(g.dma_start(out=tbsc[:, :],
                    in_=tabs_h[ti].rearrange("(t p) c -> p t c", p=128)), 16)
                v.wait_ge(s_lg_h, s_lg.val)
                vwait()
                qk = 0 if ti < 2 else DH
                for tt in range(NTL):
                    o0 = (ti * NTL + tt) * DH
                    if ti % 2 == 0:
                        s_ve.inc(v.tensor_tensor(
                            out=tabc[:, o0:o0 + DH],
                            in0=tbsc[:, tt * DH:(tt + 1) * DH],
                            in1=qsr[:, qk:qk + DH], op=ALU.mult))
                    else:
                        s_ve.inc(v.tensor_tensor(
                            out=tabc[:, o0:o0 + 32],
                            in0=tbsc[:, tt * DH:tt * DH + 32],
                            in1=qsr[:, qk + 32:qk + 64], op=ALU.mult))
                        s_ve.inc(v.tensor_tensor(
                            out=tabc[:, o0 + 32:o0 + DH],
                            in0=tbsc[:, tt * DH + 32:(tt + 1) * DH],
                            in1=qsr[:, qk:qk + 32], op=ALU.mult))
                barrier()
            build_xdm()

            for hg in range(4):
                g.wait_ge(s_lg_h, s_lg.val)
                g.wait_ge(s_mm_h, s_mm.val)
                g.wait_ge(s_tr_h, s_tr.val)
                s_lg.inc(g.dma_start(out=wqs[:, :],
                    in_=fwqkv_h[li, :, hg].rearrange("(kt p) c -> p kt c",
                                                     p=128)), 16)
                barrier()
                for tt in range(NTL):
                    for comp in range(3):
                        idx = tt * 3 + comp
                        bank = idx % 2
                        if idx >= 2:
                            te.wait_ge(s_sc_h, s_sc.val - 1)
                        for kp in range(8):
                            mm = te.matmul(
                                ps_a[:, bank * 512: bank * 512 + 256],
                                xs_slice(kp, tt * 128, (tt + 1) * 128),
                                wqs[:, kp * 768 + comp * 256:
                                    kp * 768 + (comp + 1) * 256],
                                start=(kp == 0), stop=(kp == 7))
                        s_mm.inc(mm)
                        sc.wait_ge(s_mm_h, s_mm.val)
                        dst = (qL, kL, vL)[comp]
                        s_sc.inc(sc.activation(
                            out=dst[:, tt * 256:(tt + 1) * 256],
                            in_=ps_a[:, bank * 512: bank * 512 + 256],
                            func=AF.Identity, bias=0.0, scale=1.0))
                barrier()
                if dbg == "xdm" and li == 0 and hg == 0:
                    sync.wait_ge(s_st_h, s_st.val)
                    s_st.inc(sync.dma_start(out=dbg_h[:, 0:13824],
                        in_=xdm[:, 0:13824]), 16)
                    barrier()
                if dbg == "qkv" and li == 0 and hg == 0:
                    for bi, buf in enumerate((qL, kL, vL)):
                        sync.wait_ge(s_st_h, s_st.val)
                        s_st.inc(sync.dma_start(out=dbg_h[:, bi * 4608:(bi + 1) * 4608],
                            in_=buf[:, :]), 16)
                    barrier()
                for (srcL, tb) in ((qL, 0), (kL, 2)):
                    for tt in range(NTL):
                        s0 = srcL[:, tt * 256:(tt + 1) * 256]
                        s0v = s0.rearrange("p (h c) -> p h c", c=DH)
                        vwait()
                        s_ve.inc(v.tensor_tensor(out=psc[:, 0:256], in0=s0,
                                                 in1=s0, op=ALU.mult))
                        vwait()
                        s_ve.inc(v.tensor_reduce(
                            out=pcol[:, 0:4],
                            in_=psc[:, 0:256].rearrange("p (h c) -> p h c",
                                                        c=DH),
                            axis=AX.X, op=ALU.add))
                        sc.wait_ge(s_ve_h, s_ve.val)
                        s_sc.inc(sc.activation(out=pcol[:, 4:8],
                                               in_=pcol[:, 0:4],
                                               func=AF.Sqrt, bias=0.0,
                                               scale=1.0))
                        v.wait_ge(s_sc_h, s_sc.val)
                        s_ve.inc(v.tensor_scalar_max(out=pcol[:, 4:8],
                                                     in0=pcol[:, 4:8],
                                                     scalar1=1e-12))
                        vwait()
                        s_ve.inc(v.reciprocal(out=pcol[:, 8:12],
                                              in_=pcol[:, 4:8]))
                        vwait()
                        s_ve.inc(v.tensor_tensor(
                            out=psc[:, 0:256].rearrange("p (h c) -> p h c",
                                                        c=DH),
                            in0=s0v,
                            in1=pcol[:, 8:12].unsqueeze(2)
                                .broadcast_to([128, 4, DH]),
                            op=ALU.mult))
                        ca = tabc[:, (tb * NTL + tt) * DH:
                                  (tb * NTL + tt + 1) * DH]
                        cb = tabc[:, ((tb + 1) * NTL + tt) * DH:
                                  ((tb + 1) * NTL + tt + 1) * DH]
                        vwait()
                        s_ve.inc(v.tensor_tensor(
                            out=psc[:, 256:512].rearrange("p (h c) -> p h c",
                                                          c=DH),
                            in0=psc[:, 0:256].rearrange("p (h c) -> p h c",
                                                        c=DH),
                            in1=ca.unsqueeze(1).broadcast_to([128, 4, DH]),
                            op=ALU.mult))
                        pb = psc[:, 0:256]
                        rot = bass.AP(
                            pb.tensor, pb.offset + 32,
                            [[pb.ap[0][0], 128], [DH, 4], [-32, 2], [1, 32]])
                        vwait()
                        s_ve.inc(v.tensor_tensor(
                            out=psc[:, 512:768].rearrange("p (h c) -> p h c",
                                                          c=DH),
                            in0=rot,
                            in1=cb.unsqueeze(1).broadcast_to([128, 4, DH]),
                            op=ALU.mult))
                        vwait()
                        s_ve.inc(v.tensor_tensor(
                            out=s0, in0=psc[:, 256:512],
                            in1=psc[:, 512:768], op=ALU.add))
                barrier()
                if dbg == "prep" and li == 0 and hg == 0:
                    for bi, buf in enumerate((qL, kL, vL)):
                        sync.wait_ge(s_st_h, s_st.val)
                        s_st.inc(sync.dma_start(out=dbg_h[:, bi * 4608:(bi + 1) * 4608],
                            in_=buf[:, :]), 16)
                    barrier()
                for (srcL, dstD) in ((qL, qdm), (kL, kdm)):
                    for tt in range(NTL):
                        for hp in range(2):
                            sync.wait_ge(s_tr_h, s_tr.val)
                            s_tr.inc(sync.dma_start(out=dstD[:, hp * LT + tt * 128:
                                        hp * LT + (tt + 1) * 128],
                                in_=srcL[:, tt * 256 + hp * 128:
                                         tt * 256 + (hp + 1) * 128],
                                transpose=True), 16)
                barrier()
                for hp in range(2):
                  for hh in range(2):
                    sm_sc = {}
                    tr_ct = {}
                    av_mm = {}
                    tiles = [(w, tqt) for w in range(1, 9) for tqt in range(2)]
                    for i, (w, tqt) in enumerate(tiles):
                        par = i % 2
                        if i >= 2:
                            te.wait_ge(s_sc_h, sm_sc[i - 2])
                        q0 = hp * LT + (w * 256 + tqt * 128)
                        k0 = hp * LT + (w - 1) * 256
                        te.matmul(
                            ps_b[:, par * 512:(par + 1) * 512],
                            qdm[hh * 64:(hh + 1) * 64, q0:q0 + 128],
                            kdm[hh * 64:(hh + 1) * 64, k0:k0 + 512],
                            start=True, stop=False, skip_group_check=True)
                        mvar = 0 if w == 1 else 1
                        mm = te.matmul(
                            ps_b[:, par * 512:(par + 1) * 512],
                            id128[:, :],
                            masks[:, (mvar * 2 + tqt) * 512:
                                  (mvar * 2 + tqt + 1) * 512],
                            start=False, stop=True, skip_group_check=True)
                        s_mm.inc(mm)
                        v.wait_ge(s_mm_h, s_mm.val)
                        vwait()
                        s_ve.inc(v.tensor_reduce(
                            out=nmx[:, par:par + 1],
                            in_=ps_b[:, par * 512:(par + 1) * 512],
                            axis=AX.X, op=ALU.max, negate=True))
                        sc.wait_ge(s_ve_h, s_ve.val)
                        if i >= 2:
                            sc.wait_ge(s_tr_h, tr_ct[i - 2])
                        s_sc.inc(sc.activation(
                            out=abuf[:, par * 512:(par + 1) * 512],
                            in_=ps_b[:, par * 512:(par + 1) * 512],
                            func=AF.Exp, bias=nmx[:, par:par + 1],
                            scale=1.0, accum_out=den[:, par:par + 1]))
                        sm_sc[i] = s_sc.val
                        v.wait_ge(s_sc_h, s_sc.val)
                        vwait()
                        s_ve.inc(v.reciprocal(out=rde[:, par:par + 1],
                                              in_=den[:, par:par + 1]))
                        vwait()
                        s_ve.inc(v.tensor_scalar_mul(
                            out=abuf[:, par * 512:(par + 1) * 512],
                            in0=abuf[:, par * 512:(par + 1) * 512],
                            scalar1=rde[:, par:par + 1]))
                        sync.wait_ge(s_ve_h, s_ve.val)
                        if i >= 2:
                            sync.wait_ge(s_mm_h, av_mm[i - 2])
                        for ci in range(4):
                            sync.wait_ge(s_tr_h, s_tr.val)
                            s_tr.inc(sync.dma_start(out=atd[:, par * 512 + ci * 128:
                                        par * 512 + (ci + 1) * 128],
                                in_=abuf[:, par * 512 + ci * 128:
                                         par * 512 + (ci + 1) * 128],
                                transpose=True), 16)
                        tr_ct[i] = s_tr.val
                        te.wait_ge(s_tr_h, s_tr.val)
                        oc0 = (w - 1) * 256 + tqt * 128
                        for ci in range(4):
                            tkt = 2 * (w - 1) + ci
                            mm = te.matmul(
                                ps_av[hh * 64:(hh + 1) * 64,
                                      oc0:oc0 + 128],
                                vL[:, tkt * 256 + (hp * 2 + hh) * 64:
                                   tkt * 256 + (hp * 2 + hh) * 64 + 64],
                                atd[:, par * 512 + ci * 128:
                                    par * 512 + (ci + 1) * 128],
                                start=(ci == 0), stop=(ci == 3),
                                skip_group_check=True)
                        s_mm.inc(mm)
                        av_mm[i] = s_mm.val
                    barrier()
                    if hh == 0:
                        continue
                    sc.wait_ge(s_mm_h, s_mm.val)
                    pr = (hg % 2) * 2 + hp
                    s_sc.inc(sc.activation(
                        out=att[:, pr * T2:(pr + 1) * T2],
                        in_=ps_av[:, :], func=AF.Identity, bias=0.0,
                        scale=1.0))
                    barrier()
                if hp == 1 and hg % 2 == 1:
                    load_wproj_wo(li, hg // 2)
                    barrier()
                    proj_half(with_bias=False)
                    barrier()
            if stop_after == ("a", li):
                return

            build_xdm()
            sync.wait_ge(s_ld_h, s_ld.val)
            s_ld.inc(sync.dma_start(out=gcolb[:, :], in_=flng_h[li]), 16)
            sync.wait_ge(s_ld_h, s_ld.val)
            s_ld.inc(sync.dma_start(out=bcolb[:, :], in_=flnb_h[li]), 16)
            sync.wait_ge(s_ld_h, s_ld.val)
            s_ld.inc(sync.dma_start(out=b1c[:, :], in_=fb1_h[li]), 16)
            g.wait_ge(s_lg_h, s_lg.val)
            s_lg.inc(g.dma_start(out=brow[:, :], in_=fb2_h[li]), 16)
            barrier()
            NCK = 22
            cw_last = DIM_INNER - 21 * 128
            for tb in range(4):
                tc0 = tb * 512
                ns = tb
                c0 = HALO + ns * 512
                for kp in range(8):
                    te.matmul(ps_a[0:1, 0:512], onesc[:, :],
                              xs_slice(kp, c0, c0 + 512),
                              start=(kp == 0), stop=(kp == 7),
                              skip_group_check=True)
                for kp in range(8):
                    sc.wait_ge(s_tr_h, s_tr.val)
                    if kp > 0:
                        sc.wait_ge(s_mm_h, s_mm.val)
                    s_sc.inc(sc.activation(out=sq_sc[:, :],
                                           in_=xs_slice(kp, c0, c0 + 512),
                                           func=AF.Square, bias=0.0,
                                           scale=1.0))
                    te.wait_ge(s_sc_h, s_sc.val)
                    mm = te.matmul(ps_a[0:1, 512:1024], onesc[:, :],
                                   sq_sc[:, :], start=(kp == 0),
                                   stop=(kp == 7), skip_group_check=True)
                    s_mm.inc(mm)
                barrier()
                s_ve.inc(v.tensor_scalar_mul(out=srow[0:1, :],
                                             in0=ps_a[0:1, 0:512],
                                             scalar1=1.0 / D))
                vwait()
                s_ve.inc(v.tensor_tensor(out=srow[32:33, :],
                                         in0=srow[0:1, :], in1=srow[0:1, :],
                                         op=ALU.mult))
                vwait()
                s_ve.inc(v.scalar_tensor_tensor(
                    out=srow[32:33, :], in0=ps_a[0:1, 512:1024],
                    scalar=1.0 / D, in1=srow[32:33, :],
                    op0=ALU.mult, op1=ALU.subtract))
                sc.wait_ge(s_ve_h, s_ve.val)
                s_sc.inc(sc.activation(out=srow[32:33, :], in_=srow[32:33, :],
                                       func=AF.Sqrt, bias=pcol[0:1, 22:23],
                                       scale=1.0))
                v.wait_ge(s_sc_h, s_sc.val)
                s_ve.inc(v.reciprocal(out=srow[32:33, :], in_=srow[32:33, :]))
                barrier()
                g.wait_ge(s_lg_h, s_lg.val)
                s_lg.inc(g.dma_start(out=rrep[:, 0:512],
                                     in_=srow[0:1, :].unsqueeze(1).broadcast_to([1, 128, 512])), 16)
                g.wait_ge(s_lg_h, s_lg.val)
                s_lg.inc(g.dma_start(out=rrep[:, 512:1024],
                                     in_=srow[32:33, :].unsqueeze(1)
                                     .broadcast_to([1, 128, 512])), 16)
                barrier()
                for kp in range(8):
                    vwait()
                    s_ve.inc(v.tensor_tensor(
                        out=psc[:, 0:512], in0=xs_slice(kp, c0, c0 + 512),
                        in1=rrep[:, 0:512], op=ALU.subtract))
                    vwait()
                    s_ve.inc(v.scalar_tensor_tensor(
                        out=psc[:, 512:1024], in0=psc[:, 0:512],
                        scalar=gcolb[:, kp:kp + 1], in1=rrep[:, 512:1024],
                        op0=ALU.mult, op1=ALU.mult))
                    vwait()
                    s_ve.inc(v.tensor_scalar_add(
                        out=xnb[:, kp * 512:(kp + 1) * 512],
                        in0=psc[:, 512:1024], scalar1=bcolb[:, kp:kp + 1]))
                barrier()
                for ocp in range(NCK):
                    cw = 128 if ocp < 21 else cw_last
                    g.wait_ge(s_lg_h, s_lg.val)
                    g.wait_ge(s_mm_h, s_mm.val)
                    s_lg.inc(g.dma_start(out=w1s[:, 0:8 * cw],
                        in_=fw1_h[li, :, ocp * 128: ocp * 128 + cw]
                        .rearrange("(kt p) c -> p kt c", p=128)), 16)
                    g.wait_ge(s_lg_h, s_lg.val)
                    s_lg.inc(g.dma_start(out=w1s[:, 8 * 128:8 * 128 + 8 * cw],
                        in_=fw1_h[li, :, DIM_INNER + ocp * 128:
                                  DIM_INNER + ocp * 128 + cw]
                        .rearrange("(kt p) c -> p kt c", p=128)), 16)
                    te.wait_ge(s_lg_h, s_lg.val)
                    if tb + ocp > 0:
                        te.wait_ge(s_ve_h, s_ve.val)
                    for which in range(2):
                        for kp in range(8):
                            mm = te.matmul(
                                ps_a[0:cw, which * 512:(which + 1) * 512],
                                w1s[:, which * 8 * 128 + kp * cw:
                                    which * 8 * 128 + (kp + 1) * cw],
                                xnb[:, kp * 512:(kp + 1) * 512],
                                start=(kp == 0), stop=(kp == 7))
                        s_mm.inc(mm)
                    sc.wait_ge(s_mm_h, s_mm.val)
                    s_sc.inc(sc.activation(
                        out=ffd[0:cw, ocp * 512:(ocp + 1) * 512],
                        in_=ps_a[0:cw, 0:512], func=AF.Identity,
                        bias=b1c[0:cw, ocp:ocp + 1], scale=1.0))
                    s_sc.inc(sc.activation(
                        out=sq_sc[0:cw, :],
                        in_=ps_a[0:cw, 512:1024], func=AF.Gelu,
                        bias=b1c[0:cw, 22 + ocp:23 + ocp], scale=1.0))
                    v.wait_ge(s_sc_h, s_sc.val)
                    vwait()
                    s_ve.inc(v.tensor_tensor(
                        out=ffd[0:cw, ocp * 512:(ocp + 1) * 512],
                        in0=ffd[0:cw, ocp * 512:(ocp + 1) * 512],
                        in1=sq_sc[0:cw, :], op=ALU.mult))
                for oc in range(2):
                    g.wait_ge(s_mm_h, s_mm.val)
                    for ocp in range(NCK):
                        cw = 128 if ocp < 21 else cw_last
                        g.wait_ge(s_lg_h, s_lg.val)
                        s_lg.inc(g.dma_start(out=w2s[0:cw, ocp * 512:(ocp + 1) * 512],
                            in_=fw2_h[li, ocp * 128: ocp * 128 + cw,
                                      oc * 512:(oc + 1) * 512]), 16)
                    barrier()
                    for ttin in range(4):
                        tt = tb * 4 + ttin
                        bank = ttin % 2
                        if ttin >= 2:
                            te.wait_ge(s_ve_h, s_ve.val - 1)
                        for ocp in range(NCK):
                            cw = 128 if ocp < 21 else cw_last
                            te.matmul(
                                ps_b[:, bank * 512:(bank + 1) * 512],
                                ffd[0:cw, ocp * 512 + ttin * 128:
                                    ocp * 512 + ttin * 128 + 128],
                                w2s[0:cw, ocp * 512:(ocp + 1) * 512],
                                start=(ocp == 0), stop=False,
                                skip_group_check=True)
                        mm = te.matmul(ps_b[:, bank * 512:(bank + 1) * 512],
                                       ones1[:, :],
                                       brow[:, oc * 512:(oc + 1) * 512],
                                       start=False, stop=True,
                                       skip_group_check=True)
                        s_mm.inc(mm)
                        v.wait_ge(s_mm_h, s_mm.val)
                        vwait()
                        s_ve.inc(v.tensor_tensor(
                            out=xown[:, tt * D + oc * 512:
                                     tt * D + (oc + 1) * 512],
                            in0=xown[:, tt * D + oc * 512:
                                     tt * D + (oc + 1) * 512],
                            in1=ps_b[:, bank * 512:(bank + 1) * 512],
                            op=ALU.add))
                    barrier()

        done = stop_after == "bridge"
        if not done:
            for li in range(nlayer):
                halo_exchange()
                emit_layer(li)
                if stop_after in (("f", li), ("a", li)):
                    done = True
                    break

        if not done:
          for tt in range(NTO):
            sync.wait_ge(s_ld_h, s_ld.val)
            s_ld.inc(sync.dma_start(out=psc[:, 0:512],
                in_=xmid_st.ap()[tt * 128:(tt + 1) * 128, 0:512]), 16)
            sync.wait_ge(s_ld_h, s_ld.val)
            s_ld.inc(sync.dma_start(out=psc[:, 512:1024],
                in_=xmid_st.ap()[tt * 128:(tt + 1) * 128, 512:1024]), 16)
            v.wait_ge(s_ld_h, s_ld.val)
            vwait()
            s_ve.inc(v.tensor_tensor(
                out=xown[:, tt * D: (tt + 1) * D],
                in0=xown[:, tt * D: (tt + 1) * D],
                in1=psc[:, 0:1024], op=ALU.add))
            barrier()
        if not done:
            ln_inplace()
        barrier()
        for tt in range(NTO):
            sync.wait_ge(s_st_h, s_st.val)
            s_st.inc(sync.dma_start(out=out_h[tt * 128:(tt + 1) * 128, :],
                                    in_=xown[:, tt * D:(tt + 1) * D]), 16)
        sync.wait_ge(s_st_h, s_st.val)

    es_rest.close()
    es.close()
    return nc



def mixer_inputs_for_core(inputs, b, gidx, n_heads=HG, Lm=L):
    x = np.ascontiguousarray(inputs["L0_attn_x"][b, :Lm]).astype(np.float32)
    Wq = inputs["L0_attn_Wqkv_w"]
    bq = inputs["L0_attn_Wqkv_b"]
    h0 = gidx * n_heads
    rows, brows = [], []
    for c in range(3):
        sl = slice(c * D + h0 * DH, c * D + (h0 + n_heads) * DH)
        rows.append(Wq[sl])
        brows.append(bq[sl])
    Wslice = np.concatenate(rows, axis=0)
    bslice = np.concatenate(brows, axis=0)
    lam = inputs["L0_attn_Lambda_raw"][h0:h0 + n_heads]
    gam = inputs["L0_attn_Gamma"][h0:h0 + n_heads]
    wst = inputs["L0_attn_W_static"][h0:h0 + n_heads]
    lam_c = np.ascontiguousarray(lam.reshape(n_heads, NJT, 128).transpose(0, 2, 1))
    gam_c = np.ascontiguousarray(gam.reshape(n_heads, NJT, 128).transpose(0, 2, 1))
    selw = np.zeros((128, 128), dtype=ml_dtypes.bfloat16)
    selw[np.arange(128), 64 + np.arange(128) // 64] = 1
    nmt = 3 * n_heads * DH // 128
    return {
        "mx_x": x,
        "mx_wqkvT": np.ascontiguousarray(Wslice.T).astype(np.float32),
        "mx_bqkv": np.ascontiguousarray(bslice.reshape(nmt, 128).T).astype(np.float32),
        "mx_lam": lam_c.astype(np.float32),
        "mx_gam": gam_c.astype(np.float32),
        "mx_wstT": np.ascontiguousarray(wst.transpose(0, 2, 1)).astype(np.float32),
        "mx_selw": selw,
    }


def _rot_tables(g):
    n = L
    t_loc = np.arange(g * T2 - HALO, g * T2 + T2)
    t = np.clip(t_loc, 0, n - 1).astype(np.float64)
    half = np.arange(0, DH, 2, dtype=np.float64)
    inv_freq = 1.0 / 10000.0 ** (half / DH)
    fr = t[:, None] * inv_freq[None, :]
    f = np.concatenate([fr, fr], axis=1)
    base = (half + 0.4 * DH) / (1.4 * DH)
    power = (t - n // 2) / (WINDOW // 2)
    scl = base[None, :] ** power[:, None]
    scl = np.concatenate([scl, scl], axis=1)
    sgn = np.where(np.arange(DH) < DH // 2, -1.0, 1.0)[None, :]
    ca_q = np.cos(f) * scl
    cb_q = sgn * np.sin(f) * scl
    ca_k = np.cos(f) / scl
    cb_k = sgn * np.sin(f) / scl
    return np.stack([ca_q, cb_q, ca_k, cb_k]).astype(np.float32)


def _masks_for_core(g):
    out = np.zeros((2, 2, 128, 512), dtype=np.float32)
    jj = np.arange(512)[None, :]
    for var in range(2):
        for tqt in range(2):
            tau = (tqt * 128 + np.arange(128))[:, None]
            valid = (jj >= tau) & (jj <= tau + 256)
            if var == 0 and g == 0:
                valid = valid & (jj >= 256)
            out[var, tqt] = np.where(valid, 0.0, -1e9)
    return out.astype(ml_dtypes.bfloat16)


def rest_inputs_for_core(inputs, b, g):
    W = inputs["L0_ffn_Wqkv"].astype(np.float32)
    fwqkv = np.ascontiguousarray(
        W.reshape(NLAYER, D, 3, 4, 4 * DH).transpose(0, 1, 3, 2, 4)
        .reshape(NLAYER, D, 4, 768))
    b1 = inputs["L0_ffn_b1"].astype(np.float32)
    b1c = np.zeros((NLAYER, 128, 44), dtype=np.float32)
    for ocp in range(22):
        cw = 128 if ocp < 21 else DIM_INNER - 21 * 128
        b1c[:, :cw, ocp] = b1[:, ocp * 128: ocp * 128 + cw]
        b1c[:, :cw, 22 + ocp] = b1[:, DIM_INNER + ocp * 128:
                                   DIM_INNER + ocp * 128 + cw]
    lng = np.ascontiguousarray(
        inputs["L0_ffn_ln_g"].astype(np.float32).reshape(NLAYER, 8, 128)
        .transpose(0, 2, 1))
    lnb = np.ascontiguousarray(
        inputs["L0_ffn_ln_b"].astype(np.float32).reshape(NLAYER, 8, 128)
        .transpose(0, 2, 1))
    hg2 = np.zeros((128, 2), dtype=np.float32)
    hg2[:, 0] = float(g)
    hg2[:, 1] = 1.0 - float(g)
    return {
        "r_x": np.ascontiguousarray(
            inputs["L0_attn_x"][b, g * T2:(g + 1) * T2]).astype(np.float32),
        "r_outwT": np.ascontiguousarray(
            inputs["L0_attn_out_w"].astype(np.float32).T),
        "r_outb": inputs["L0_attn_out_b"].astype(np.float32).reshape(1, D),
        "r_fwqkv": fwqkv,
        "r_fwo": np.ascontiguousarray(inputs["L0_ffn_Wo"].astype(np.float32)),
        "r_fw1": np.ascontiguousarray(inputs["L0_ffn_W1"].astype(np.float32)),
        "r_fw2": np.ascontiguousarray(inputs["L0_ffn_W2"].astype(np.float32)),
        "r_fb1": b1c,
        "r_fb2": inputs["L0_ffn_b2"].astype(np.float32).reshape(NLAYER, 1, D),
        "r_flng": lng,
        "r_flnb": lnb,
        "r_fqs": inputs["L0_ffn_q_scale"].astype(np.float32).reshape(NLAYER, 1, DH),
        "r_fks": inputs["L0_ffn_k_scale"].astype(np.float32).reshape(NLAYER, 1, DH),
        "r_tabs": _rot_tables(g),
        "r_mask": _masks_for_core(g),
        "r_id128": np.eye(128, dtype=ml_dtypes.bfloat16),
        "r_ones": np.ones((1, 128), dtype=ml_dtypes.bfloat16),
        "r_hgate": hg2,
    }


def kernel(**inputs):
    nc = build_fused(nlayer=NLAYER)
    in_maps = []
    for c in range(8):
        b, g = c // 2, c % 2
        m = mixer_inputs_for_core(inputs, b, g)
        m.update(rest_inputs_for_core(inputs, b, g))
        in_maps.append(m)
    res = run_bass_kernel_spmd(nc, in_maps, core_ids=list(range(8)))

    out = np.empty((B, L, D), dtype=np.float32)
    for c in range(8):
        b, g = c // 2, c % 2
        out[b, g * T2:(g + 1) * T2] = res.results[c]["r_out"]
    return out


# version 07 (final)
# speedup vs baseline: 1.8790x; 1.8790x over previous
import numpy as np
import ml_dtypes
import concourse.bass as bass
import concourse.mybir as mybir
from concourse.bass_utils import run_bass_kernel_spmd

F32 = mybir.dt.float32
BF16 = mybir.dt.bfloat16
ALU = mybir.AluOpType
AF = mybir.ActivationFunctionType
AX = mybir.AxisListType

B, L, D, H = 4, 4096, 1024, 16
DH = 64
HG = 8
NJT = 32
WINDOW = 256
NLAYER = 6
DIM_FF2 = 5460
DIM_INNER = 2730
QK_SCALE = 8.0

TC = 1024
NCH = L // TC
NT = TC // 128
NS = TC // 512
NMT = 3 * HG * DH // 128
NHP = HG // 2

T2 = 2048
HALO = 256
NTH = HALO // 128
NTL = NTH + 16
NTO = 16
LT = HALO + T2
NXB = 3


class Sem:

    def __init__(self, h):
        self.h = h
        self.val = 0

    def inc(self, inst, n=1):
        inst.then_inc(self.h, n)
        self.val += n
        return self.val


def build_fused(nlayer=NLAYER, stop_after=None, dbg=None):
    nc = bass.Bass("TRN2", target_bir_lowering=False)

    x_h = nc.dram_tensor("mx_x", [L, D], BF16, kind="ExternalInput").ap()
    wqkvT_h = nc.dram_tensor("mx_wqkvT", [D, 3 * HG * DH], BF16,
                             kind="ExternalInput").ap()
    bqkv_h = nc.dram_tensor("mx_bqkv", [128, NMT], F32,
                            kind="ExternalInput").ap()
    lam_h = nc.dram_tensor("mx_lam", [HG, 128, NJT], F32,
                           kind="ExternalInput").ap()
    gam_h = nc.dram_tensor("mx_gam", [HG, 128, NJT], F32,
                           kind="ExternalInput").ap()
    wstT_h = nc.dram_tensor("mx_wstT", [HG, DH, DH], F32,
                            kind="ExternalInput").ap()
    selw_h = nc.dram_tensor("mx_selw", [128, 128], BF16,
                            kind="ExternalInput").ap()

    x_hh = nc.dram_tensor("r_x", [T2, D], F32, kind="ExternalInput").ap()
    outwT_h = nc.dram_tensor("r_outwT", [D, D], BF16, kind="ExternalInput").ap()
    outb_h = nc.dram_tensor("r_outb", [1, D], F32, kind="ExternalInput").ap()
    wqt_h = nc.dram_tensor("r_wqt", [NLAYER, 4, 128, 8 * 768], BF16,
                           kind="ExternalInput").ap()
    fwo_h = nc.dram_tensor("r_fwo", [NLAYER, D, D], BF16, kind="ExternalInput").ap()
    w1t_h = nc.dram_tensor("r_w1t", [NLAYER, 128, 43680], BF16,
                           kind="ExternalInput").ap()
    w2t_h = nc.dram_tensor("r_w2t", [NLAYER, 128, 22528], BF16,
                           kind="ExternalInput").ap()
    fb1_h = nc.dram_tensor("r_fb1", [NLAYER, 128, 44], F32,
                           kind="ExternalInput").ap()
    fb2_h = nc.dram_tensor("r_fb2", [NLAYER, 1, D], F32, kind="ExternalInput").ap()
    flng_h = nc.dram_tensor("r_flng", [NLAYER, 128, 8], F32,
                            kind="ExternalInput").ap()
    flnb_h = nc.dram_tensor("r_flnb", [NLAYER, 128, 8], F32,
                            kind="ExternalInput").ap()
    fqs_h = nc.dram_tensor("r_fqs", [NLAYER, 1, DH], F32, kind="ExternalInput").ap()
    fks_h = nc.dram_tensor("r_fks", [NLAYER, 1, DH], F32, kind="ExternalInput").ap()
    tabt_h = nc.dram_tensor("r_tabt", [4, 128, NTL * DH], BF16,
                            kind="ExternalInput").ap()
    mask_h = nc.dram_tensor("r_mask", [2, 2, 128, 512], BF16,
                            kind="ExternalInput").ap()
    id_h = nc.dram_tensor("r_id128", [128, 128], BF16, kind="ExternalInput").ap()
    ones_h = nc.dram_tensor("r_ones", [1, 128], BF16, kind="ExternalInput").ap()
    hgate_h = nc.dram_tensor("r_hgate", [128, 2], F32, kind="ExternalInput").ap()
    out_h = nc.dram_tensor("r_out", [T2, D], F32, kind="ExternalOutput").ap()
    dbg_h = nc.dram_tensor("r_dbg", [128, 3 * 4608], BF16, kind="ExternalOutput").ap()

    y2in = nc.dram_tensor("y2in", [4 * 128, L], BF16)
    y2out = nc.dram_tensor("y2out", [8 * 128, L], BF16)
    xmid_st = nc.dram_tensor("xmid_stash", [T2, D], F32)
    hx_in = nc.dram_tensor("hx_in", [NXB * 128, D], BF16)
    hx_out = nc.dram_tensor("hx_out", [2 * NXB * 128, D], BF16)

    from contextlib import ExitStack
    es = ExitStack()
    s_ld_h = es.enter_context(nc.semaphore("s_ld"))
    s_lg_h = es.enter_context(nc.semaphore("s_lg"))
    s_mm_h = es.enter_context(nc.semaphore("s_mm"))
    s_ve_h = es.enter_context(nc.semaphore("s_ve"))
    s_sc_h = es.enter_context(nc.semaphore("s_sc"))
    s_tr_h = es.enter_context(nc.semaphore("s_tr"))
    s_st_h = es.enter_context(nc.semaphore("s_st"))
    s_cc_h = es.enter_context(nc.semaphore("s_cc"))
    s_w1_h = es.enter_context(nc.semaphore("s_w1"))
    s_ld, s_lg, s_mm = Sem(s_ld_h), Sem(s_lg_h), Sem(s_mm_h)
    s_ve, s_sc, s_tr = Sem(s_ve_h), Sem(s_sc_h), Sem(s_tr_h)
    s_st, s_cc, s_w1 = Sem(s_st_h), Sem(s_cc_h), Sem(s_w1_h)

    es_mix = ExitStack()
    xl = es_mix.enter_context(nc.sbuf_tensor("m_xl", [128, NT * D], BF16))
    mxdm = es_mix.enter_context(nc.sbuf_tensor("m_xdm", [128, 8 * TC], BF16))
    wq = es_mix.enter_context(nc.sbuf_tensor("m_wq", [128, 8 * 3 * HG * DH], BF16))
    bq = es_mix.enter_context(nc.sbuf_tensor("m_bq", [128, NMT], F32))
    qkv = es_mix.enter_context(nc.sbuf_tensor("m_qkv", [128, NMT * TC], BF16))
    q2 = es_mix.enter_context(nc.sbuf_tensor("m_q2", [128, 2 * TC], BF16))
    k2 = es_mix.enter_context(nc.sbuf_tensor("m_k2", [128, 2 * TC], BF16))
    vrep = es_mix.enter_context(nc.sbuf_tensor("m_vrep", [128, 2 * 2 * TC], BF16))
    ubuf = es_mix.enter_context(nc.sbuf_tensor("m_ubuf", [128, 2 * TC], BF16))
    straj = es_mix.enter_context(nc.sbuf_tensor("m_straj", [128, 2 * TC], BF16))
    atl = es_mix.enter_context(nc.sbuf_tensor("m_atl", [128, 2 * 2 * TC], BF16))
    rcol = es_mix.enter_context(nc.sbuf_tensor("m_rcol", [128, HG * NJT], F32))
    gcol = es_mix.enter_context(nc.sbuf_tensor("m_gcol", [128, HG * NJT], F32))
    scar = es_mix.enter_context(nc.sbuf_tensor("m_scar", [128, HG * NJT], F32))
    wst = es_mix.enter_context(nc.sbuf_tensor("m_wst", [128, NHP * 128], BF16))
    selw = es_mix.enter_context(nc.sbuf_tensor("m_selw", [128, 128], BF16))
    ydm = es_mix.enter_context(nc.sbuf_tensor("m_ydm", [128, NHP * L], BF16))
    ps_qkv = es_mix.enter_context(nc.psum_tensor("m_ps_qkv", [128, 2 * 512], F32))
    ps_y = es_mix.enter_context(nc.psum_tensor("m_ps_y", [128, 2 * 512], F32))

    block = es.enter_context(nc.Block())
    es_rest = ExitStack()

    @block.sync
    def _(sync):
        v = nc.vector
        sc = nc.scalar
        te = nc.tensor
        g = nc.gpsimd

        def barrier():
            for eng in (v, sc, te, g, sync):
                eng.wait_ge(s_ld_h, s_ld.val)
                eng.wait_ge(s_lg_h, s_lg.val)
                eng.wait_ge(s_tr_h, s_tr.val)
                eng.wait_ge(s_st_h, s_st.val)
                eng.wait_ge(s_mm_h, s_mm.val)
                eng.wait_ge(s_ve_h, s_ve.val)
                eng.wait_ge(s_sc_h, s_sc.val)
                eng.wait_ge(s_cc_h, s_cc.val)
                eng.wait_ge(s_w1_h, s_w1.val)

        def vwait():
            v.wait_ge(s_ve_h, s_ve.val)

        for kt in range(8):
            s_lg.inc(g.dma_start(out=wq[:, kt * 3 * HG * DH:(kt + 1) * 3 * HG * DH],
                in_=wqkvT_h[kt * 128:(kt + 1) * 128, :]), 16)
        s_ld.inc(sync.dma_start(out=bq[:, :], in_=bqkv_h[:, :]), 16)
        for h in range(HG):
            s_ld.inc(sync.dma_start(out=rcol[:, h * NJT:(h + 1) * NJT],
                                    in_=lam_h[h]), 16)
            s_ld.inc(sync.dma_start(out=gcol[:, h * NJT:(h + 1) * NJT],
                                    in_=gam_h[h]), 16)
        s_ve.inc(v.memset(wst[:, :], 0.0))
        s_ld.inc(sync.dma_start(out=selw[:, :], in_=selw_h[:, :]), 16)
        g.wait_ge(s_ve_h, s_ve.val)
        for hp in range(NHP):
            s_lg.inc(g.dma_start(out=wst[0:64, hp * 128:hp * 128 + 64],
                                 in_=wstT_h[2 * hp]), 16)
            s_lg.inc(g.dma_start(out=wst[64:128, hp * 128 + 64:hp * 128 + 128],
                                 in_=wstT_h[2 * hp + 1]), 16)

        sc.wait_ge(s_ld_h, s_ld.val)
        v.wait_ge(s_ld_h, s_ld.val)
        sc.wait_ge(s_lg_h, s_lg.val)
        v.wait_ge(s_lg_h, s_lg.val)
        s_sc.inc(sc.activation(out=rcol[:, :], in_=rcol[:, :],
                               func=AF.Sigmoid, bias=0.0, scale=-1.0))
        s_ve.inc(v.tensor_scalar_mul(out=gcol[:, :], in0=gcol[:, :],
                                     scalar1=0.125))
        s_ve.inc(v.memset(scar[:, :], 0.0))
        barrier()

        def dmrow(c, h):
            gch = c * HG * DH + h * DH
            return (gch // 128, gch % 128)

        for ch in range(NCH):
            t0 = ch * TC
            for tt in range(NT):
                s_lg.inc(g.dma_start(out=xl[:, tt * D:(tt + 1) * D],
                    in_=x_h[t0 + tt * 128: t0 + (tt + 1) * 128, :]), 16)
            sync.wait_ge(s_lg_h, s_lg.val)
            for tt in range(NT):
                for dm in range(8):
                    s_tr.inc(sync.dma_start(out=mxdm[:, dm * TC + tt * 128: dm * TC + (tt + 1) * 128],
                        in_=xl[:, tt * D + dm * 128: tt * D + (dm + 1) * 128],
                        transpose=True), 16)
            te.wait_ge(s_tr_h, s_tr.val)

            evacs = []
            for m in range(NMT):
                for ns in range(NS):
                    qidx = m * NS + ns
                    bank = qidx % 2
                    if qidx >= 2:
                        te.wait_ge(s_sc_h, evacs[qidx - 2])
                    for kt in range(8):
                        mm = te.matmul(
                            ps_qkv[:, bank * 512:(bank + 1) * 512],
                            wq[:, kt * 3 * HG * DH + m * 128:
                               kt * 3 * HG * DH + (m + 1) * 128],
                            mxdm[:, kt * TC + ns * 512: kt * TC + (ns + 1) * 512],
                            start=(kt == 0), stop=(kt == 7))
                    s_mm.inc(mm)
                    sc.wait_ge(s_mm_h, s_mm.val)
                    s_sc.inc(sc.activation(
                        out=qkv[:, m * TC + ns * 512: m * TC + (ns + 1) * 512],
                        in_=ps_qkv[:, bank * 512:(bank + 1) * 512],
                        func=AF.Identity, bias=bq[:, m:m + 1], scale=1.0))
                    evacs.append(s_sc.val)
            barrier()

            for hp in range(NHP):
                hA, hB = 2 * hp, 2 * hp + 1
                for (hh, dst0) in ((hA, 0), (hB, TC)):
                    mq, rq = dmrow(0, hh)
                    mk, rk = dmrow(1, hh)
                    for half in range(2):
                        s_lg.inc(g.dma_start(out=q2[64 * half:64 * half + 64, dst0:dst0 + TC],
                            in_=qkv[rq:rq + 64, mq * TC: mq * TC + TC]), 16)
                        s_lg.inc(g.dma_start(out=k2[64 * half:64 * half + 64, dst0:dst0 + TC],
                            in_=qkv[rk:rk + 64, mk * TC: mk * TC + TC]), 16)

                mqA, rqA = dmrow(0, hA)
                for ns in range(NS):
                    bank = ns % 2
                    te.matmul(
                        ps_y[:, bank * 512:(bank + 1) * 512],
                        wst[:, hp * 128:(hp + 1) * 128],
                        qkv[:, mqA * TC + ns * 512: mqA * TC + (ns + 1) * 512],
                        start=True, stop=False, skip_group_check=True)

                jt_mm = []
                jt_ve = []
                scarcp = {}
                vrep_ld = {}

                def vrep_dma(jt):
                    par = jt % 2
                    sem = (s_lg, s_w1)[par]
                    for (hh, off) in ((hA, 0), (hB, TC)):
                        mv, rv = dmrow(2, hh)
                        src = qkv[rv + 2 * jt: rv + 2 * jt + 2,
                                  mv * TC: mv * TC + TC]
                        sem.inc(g.dma_start(out=vrep[:, par * 2 * TC + off:
                                     par * 2 * TC + off + TC],
                            in_=src.unsqueeze(1).broadcast_to([2, 64, TC])), 16)
                    vrep_ld[jt] = (sem.h, sem.val)

                vrep_dma(0)
                for jt in range(NJT):
                    par = jt % 2
                    if jt + 1 < NJT:
                        if jt >= 1:
                            g.wait_ge(s_ve_h, jt_ve[jt - 1])
                        vrep_dma(jt + 1)
                    v.wait_ge(*vrep_ld[jt])
                    if jt >= 2:
                        v.wait_ge(s_mm_h, jt_mm[jt - 2])
                    for (hh, off) in ((hA, 0), (hB, TC)):
                        hcol = hh * NJT + jt
                        v.wait_ge(s_ve_h, s_ve.val)
                        s_ve.inc(v.scalar_tensor_tensor(
                            out=ubuf[:, off:off + TC],
                            in0=k2[:, off:off + TC],
                            scalar=gcol[:, hcol:hcol + 1],
                            in1=vrep[:, par * 2 * TC + off:
                                     par * 2 * TC + off + TC],
                            op0=ALU.mult, op1=ALU.mult))
                        v.wait_ge(s_ve_h, s_ve.val)
                        if off in scarcp:
                            v.wait_ge(s_sc_h, scarcp[off])
                        s_ve.inc(v.tensor_tensor_scan(
                            out=straj[:, off:off + TC],
                            data0=rcol[:, hcol:hcol + 1].broadcast_to([128, TC]),
                            data1=ubuf[:, off:off + TC],
                            initial=scar[:, hcol:hcol + 1],
                            op0=ALU.mult, op1=ALU.add))
                        sc.wait_ge(s_ve_h, s_ve.val)
                        s_sc.inc(sc.copy(scar[:, hcol:hcol + 1],
                                         straj[:, off + TC - 1:off + TC]))
                        scarcp[off] = s_sc.val
                        v.wait_ge(s_ve_h, s_ve.val)
                        s_ve.inc(v.tensor_tensor(
                            out=atl[:, par * 2 * TC + off:
                                    par * 2 * TC + off + TC],
                            in0=straj[:, off:off + TC],
                            in1=q2[:, off:off + TC], op=ALU.mult))
                    jt_ve.append(s_ve.val)
                    te.wait_ge(s_ve_h, s_ve.val)
                    last = jt == NJT - 1
                    for ns in range(NS):
                        bank = ns % 2
                        for (hh, off, pbase) in ((hA, 0, 0), (hB, TC, 64)):
                            mm = te.matmul(
                                ps_y[pbase:pbase + 64,
                                     bank * 512:(bank + 1) * 512],
                                selw[:, 64 - 2 * jt:128 - 2 * jt],
                                atl[:, par * 2 * TC + off + ns * 512:
                                    par * 2 * TC + off + (ns + 1) * 512],
                                start=False, stop=last,
                                skip_group_check=True)
                    s_mm.inc(mm)
                    jt_mm.append(s_mm.val)

                for ns in range(NS):
                    bank = ns % 2
                    sc.wait_ge(s_mm_h, s_mm.val)
                    s_sc.inc(sc.activation(
                        out=ydm[:, hp * L + t0 + ns * 512:
                                hp * L + t0 + (ns + 1) * 512],
                        in_=ps_y[:, bank * 512:(bank + 1) * 512],
                        func=AF.Identity, bias=0.0, scale=1.0))
                barrier()

        sync.wait_ge(s_sc_h, s_sc.val)
        for hp in range(NHP):
            s_st.inc(sync.dma_start(out=y2in.ap()[hp * 128:(hp + 1) * 128, :],
                in_=ydm[:, hp * L:(hp + 1) * L]), 16)
        g.wait_ge(s_st_h, s_st.val)
        s_cc.inc(g.collective_compute(
            "AllGather", ALU.bypass,
            replica_groups=[[0, 1], [2, 3], [4, 5], [6, 7]],
            ins=[y2in.ap().opt()], outs=[y2out.ap().opt()]), 1)
        barrier()

        es_mix.close()
        xown = es_rest.enter_context(nc.sbuf_tensor("xown", [128, NTO * D], F32))
        xhalo = es_rest.enter_context(nc.sbuf_tensor("xhalo", [128, NTH * D], BF16))
        xprev = es_rest.enter_context(nc.sbuf_tensor("xprev", [1, D], BF16))
        XST = 16 + LT
        xdm = es_rest.enter_context(nc.sbuf_tensor("xdm", [128, 8 * XST], BF16))
        att = es_rest.enter_context(nc.sbuf_tensor("att", [128, 4 * T2], BF16))
        tabc = es_rest.enter_context(nc.sbuf_tensor("tabc", [128, 4 * NTL * DH], BF16))
        masks = es_rest.enter_context(nc.sbuf_tensor("masks", [128, 4 * 512], BF16))
        id128 = es_rest.enter_context(nc.sbuf_tensor("id128", [128, 128], BF16))
        ones1 = es_rest.enter_context(nc.sbuf_tensor("ones1", [1, 128], BF16))
        onesc = es_rest.enter_context(nc.sbuf_tensor("onesc", [128, 1], BF16))
        hgate = es_rest.enter_context(nc.sbuf_tensor("hgate", [128, 2], F32))
        arena = es_rest.enter_context(nc.sbuf_tensor("arena", [128, 30720], BF16))
        qL = arena[:, 0:4608]
        kL = arena[:, 4608:9216]
        vL = arena[:, 9216:13824]
        wqs = arena[:, 13824:19968]
        qdm = arena[:, 13824:18432]
        kdm = arena[:, 0:4608]
        ffd = arena[:, 0:11264]
        w2sA = arena[:, 11264:22528]
        w1s = arena[:, 22528:24576]
        w1s_b = arena[:, 24576:26624]
        rrep = tabc[:, 2048:4096].bitcast(F32)
        xnb = arena[:, 26624:30720]
        hxs = arena[:, 0:NXB * D]
        wos = arena[:, 0:4096]
        yscr = arena[:, 4096:12288]
        tbsc = arena[:, 4096:5248]
        srow = es_rest.enter_context(nc.sbuf_tensor("srow", [64, 512], F32))
        gcolb = es_rest.enter_context(nc.sbuf_tensor("gcolb", [128, 8], F32))
        bcolb = es_rest.enter_context(nc.sbuf_tensor("bcolb", [128, 8], F32))
        b1c = es_rest.enter_context(nc.sbuf_tensor("b1c", [128, 44], F32))
        brow = es_rest.enter_context(nc.sbuf_tensor("brow", [1, 1024], BF16))
        qsr = es_rest.enter_context(nc.sbuf_tensor("qsr", [128, 2 * DH], F32))
        psc = es_rest.enter_context(nc.sbuf_tensor("psc", [128, 1024], F32))
        pcol = es_rest.enter_context(nc.sbuf_tensor("pcol", [128, 32], F32))
        ln_sc = psc[:, 0:512].bitcast(BF16)
        sq_sc = ln_sc[:, 0:512]
        nmx = pcol[:, 16:18]
        den = pcol[:, 18:20]
        rde = pcol[:, 20:22]
        abuf = psc[:, 0:512].bitcast(BF16)
        atd = psc[:, 512:1024].bitcast(BF16)
        ps_a = es_rest.enter_context(nc.psum_tensor("ps_a", [128, 2 * 512], F32))
        ps_b = es_rest.enter_context(nc.psum_tensor("ps_b", [128, 2 * 512], F32))
        ps_av = es_rest.enter_context(nc.psum_tensor("ps_av", [128, 2048], F32))

        s_ld.inc(sync.dma_start(out=id128[:], in_=id_h[:, :]), 16)
        s_ld.inc(sync.dma_start(out=ones1[:], in_=ones_h[:, :]), 16)
        s_ld.inc(sync.dma_start(out=hgate[:], in_=hgate_h[:, :]), 16)
        for mi in range(4):
            s_ld.inc(sync.dma_start(out=masks[:, mi * 512:(mi + 1) * 512],
                in_=mask_h[mi // 2, mi % 2]), 16)
        s_ve.inc(v.memset(onesc[:], 1.0))
        s_ve.inc(v.memset(pcol[:, 22:23], 1e-5))
        s_ve.inc(v.memset(xhalo[:, :], 0.0))
        s_ve.inc(v.memset(xprev[:, :], 0.0))
        for tt in range(NTO):
            s_ld.inc(sync.dma_start(out=xown[:, tt * D:(tt + 1) * D],
                                    in_=x_hh[tt * 128:(tt + 1) * 128, :]), 16)
        s_lg.inc(g.dma_start(out=brow[:, :], in_=outb_h[:, :]), 16)
        barrier()

        def load_wproj_wo(li, half):
            for kp in range(4):
                s_lg.inc(g.dma_start(out=wos[:, kp * 1024:(kp + 1) * 1024],
                    in_=fwo_h[li, (half * 4 + kp) * 128:
                              (half * 4 + kp + 1) * 128, :]), 16)

        def proj_half(with_bias):
            for tt in range(NTO):
                for oc in range(2):
                    idx = tt * 2 + oc
                    bank = idx % 2
                    if idx >= 2:
                        te.wait_ge(s_ve_h, s_ve.val - 1)
                    for kp in range(4):
                        mm = te.matmul(
                            ps_a[:, bank * 512:(bank + 1) * 512],
                            att[:, kp * T2 + tt * 128:
                                kp * T2 + (tt + 1) * 128],
                            wos[:, kp * 1024 + oc * 512:
                                kp * 1024 + (oc + 1) * 512],
                            start=(kp == 0),
                            stop=(kp == 3 and not with_bias))
                    if with_bias:
                        mm = te.matmul(
                            ps_a[:, bank * 512:(bank + 1) * 512],
                            ones1[:, :], brow[:, oc * 512:(oc + 1) * 512],
                            start=False, stop=True)
                    s_mm.inc(mm)
                    v.wait_ge(s_mm_h, s_mm.val)
                    vwait()
                    s_ve.inc(v.tensor_tensor(
                        out=xown[:, tt * D + oc * 512: tt * D + (oc + 1) * 512],
                        in0=xown[:, tt * D + oc * 512: tt * D + (oc + 1) * 512],
                        in1=ps_a[:, bank * 512:(bank + 1) * 512], op=ALU.add))

        def ln_inplace():
            for tt in range(NTO):
                xt = xown[:, tt * D:(tt + 1) * D]
                vwait()
                s_ve.inc(v.tensor_reduce(out=pcol[:, 0:1], in_=xt,
                                         axis=AX.X, op=ALU.add))
                vwait()
                s_ve.inc(v.tensor_scalar_mul(out=pcol[:, 1:2],
                                             in0=pcol[:, 0:1], scalar1=1.0 / D))
                vwait()
                s_ve.inc(v.scalar_tensor_tensor(
                    out=ln_sc[:, :], in0=xt, scalar=pcol[:, 1:2], in1=xt,
                    op0=ALU.subtract, op1=ALU.mult, accum_out=pcol[:, 2:3]))
                sc.wait_ge(s_ve_h, s_ve.val)
                s_sc.inc(sc.activation(out=pcol[:, 3:4], in_=pcol[:, 2:3],
                                       func=AF.Sqrt, bias=pcol[:, 22:23],
                                       scale=1.0 / D))
                v.wait_ge(s_sc_h, s_sc.val)
                s_ve.inc(v.reciprocal(out=pcol[:, 4:5], in_=pcol[:, 3:4]))
                vwait()
                s_ve.inc(v.scalar_tensor_tensor(
                    out=xt, in0=xt, scalar=pcol[:, 1:2],
                    in1=pcol[:, 4:5].broadcast_to([128, D]),
                    op0=ALU.subtract, op1=ALU.mult))

        for half in range(2):
            blend_done = {}
            for kp in range(4):
                par = kp % 2
                scr = yscr[:, par * 4096:(par + 1) * 4096]
                if kp >= 2:
                    g.wait_ge(s_ve_h, blend_done[kp - 2])
                sem = (s_lg, s_w1)[kp % 2]
                sem.inc(g.dma_start(out=scr,
                    in_=y2out.ap()[(half * 4 + kp) * 128:
                                   (half * 4 + kp + 1) * 128, :]), 16)
                sem.inc(g.dma_start(out=wos[:, kp * 1024:(kp + 1) * 1024],
                    in_=outwT_h[(half * 4 + kp) * 128:
                                (half * 4 + kp + 1) * 128, :]), 16)
                v.wait_ge(sem.h, sem.val)
                att_t = att[:, kp * T2:(kp + 1) * T2]
                vwait()
                s_ve.inc(v.tensor_scalar_mul(out=att_t, in0=scr[:, 0:T2],
                                             scalar1=hgate[:, 1:2]))
                vwait()
                s_ve.inc(v.scalar_tensor_tensor(
                    out=att_t, in0=scr[:, T2:2 * T2], scalar=hgate[:, 0:1],
                    in1=att_t, op0=ALU.mult, op1=ALU.add))
                blend_done[kp] = s_ve.val
            barrier()
            if dbg == "att0" and half == 0:
                sync.wait_ge(s_st_h, s_st.val)
                s_st.inc(sync.dma_start(out=dbg_h[:, 0:4 * T2],
                                        in_=att[:, :]), 16)
                barrier()
            proj_half(with_bias=(half == 1))
            barrier()
        ln_inplace()
        barrier()
        for tt in range(NTO):
            s_st.inc(sync.dma_start(out=xmid_st.ap()[tt * 128:(tt + 1) * 128, :],
                                    in_=xown[:, tt * D:(tt + 1) * D]), 16)
        barrier()

        def halo_exchange():
            for i, tt in enumerate(range(13, 16)):
                s_sc.inc(sc.copy(hxs[:, i * D:(i + 1) * D],
                                 xown[:, tt * D:(tt + 1) * D]))
            sync.wait_ge(s_sc_h, s_sc.val)
            for i in range(NXB):
                s_st.inc(sync.dma_start(out=hx_in.ap()[i * 128:(i + 1) * 128, :],
                    in_=hxs[:, i * D:(i + 1) * D]), 16)
            g.wait_ge(s_st_h, s_st.val)
            s_cc.inc(g.collective_compute(
                "AllGather", ALU.bypass,
                replica_groups=[[0, 1], [2, 3], [4, 5], [6, 7]],
                ins=[hx_in.ap().opt()], outs=[hx_out.ap().opt()]), 1)
            sync.wait_ge(s_cc_h, s_cc.val)
            for i in range(NTH):
                s_ld.inc(sync.dma_start(out=xhalo[:, i * D:(i + 1) * D],
                    in_=hx_out.ap()[128 + i * 128: 128 + (i + 1) * 128, :]), 16)
            s_ld.inc(sync.dma_start(out=xprev[:, :],
                                    in_=hx_out.ap()[127:128, :]), 16)
            v.wait_ge(s_ld_h, s_ld.val)
            for i in range(NTH):
                vwait()
                s_ve.inc(v.tensor_scalar_mul(out=xhalo[:, i * D:(i + 1) * D],
                                             in0=xhalo[:, i * D:(i + 1) * D],
                                             scalar1=hgate[:, 0:1]))
            vwait()
            s_ve.inc(v.tensor_scalar_mul(out=xprev[:, :], in0=xprev[:, :],
                                         scalar1=hgate[0:1, 0:1]))
            barrier()

        def build_xdm():
            s_ve.inc(v.memset(xdm[:, :], 0.0))
            barrier()
            trmark = {}
            for tt in range(NTL):
                if tt < NTH:
                    src = xhalo[:, tt * D:(tt + 1) * D]
                else:
                    buf = (ln_sc, atd)[(tt - NTH) % 2]
                    sc.wait_ge(s_tr_h, s_tr.val)
                    s_sc.inc(sc.copy(buf[:, :],
                                     xown[:, (tt - NTH) * D:(tt - NTH + 1) * D]))
                    sync.wait_ge(s_sc_h, s_sc.val)
                    src = buf[:, :]
                for dm in range(8):
                    s_tr.inc(sync.dma_start(out=xdm[:, dm * XST + 16 + tt * 128:
                                dm * XST + 16 + (tt + 1) * 128],
                        in_=src[:, dm * 128:(dm + 1) * 128],
                        transpose=True), 16)
                trmark[tt] = s_tr.val
            for dm in range(4, 8):
                s_ld.inc(sync.dma_start(out=xdm[:, dm * XST + 15: dm * XST + 16],
                    in_=xprev[:, dm * 128:(dm + 1) * 128]), 16)
            barrier()

        def xs_slice(dm, c0, c1):
            off = dm * XST + (16 if dm < 4 else 15)
            return xdm[:, off + c0: off + c1]

        def emit_layer(li):
            s_lg.inc(g.dma_start(out=qsr[:, 0:DH],
                                 in_=fqs_h[li].broadcast_to([128, DH])), 16)
            s_lg.inc(g.dma_start(out=qsr[:, DH:2 * DH],
                                 in_=fks_h[li].broadcast_to([128, DH])), 16)
            v.wait_ge(s_lg_h, s_lg.val)
            vwait()
            s_ve.inc(v.tensor_scalar_mul(out=qsr[:, 0:DH], in0=qsr[:, 0:DH],
                                         scalar1=QK_SCALE))
            for ti in range(4):
                g.wait_ge(s_ve_h, s_ve.val)
                s_lg.inc(g.dma_start(out=tbsc[:, :],
                    in_=tabt_h[ti]), 16)
                v.wait_ge(s_lg_h, s_lg.val)
                vwait()
                qk = 0 if ti < 2 else DH
                for tt in range(NTL):
                    o0 = (ti * NTL + tt) * DH
                    if ti % 2 == 0:
                        s_ve.inc(v.tensor_tensor(
                            out=tabc[:, o0:o0 + DH],
                            in0=tbsc[:, tt * DH:(tt + 1) * DH],
                            in1=qsr[:, qk:qk + DH], op=ALU.mult))
                    else:
                        s_ve.inc(v.tensor_tensor(
                            out=tabc[:, o0:o0 + 32],
                            in0=tbsc[:, tt * DH:tt * DH + 32],
                            in1=qsr[:, qk + 32:qk + 64], op=ALU.mult))
                        s_ve.inc(v.tensor_tensor(
                            out=tabc[:, o0 + 32:o0 + DH],
                            in0=tbsc[:, tt * DH + 32:(tt + 1) * DH],
                            in1=qsr[:, qk:qk + 32], op=ALU.mult))
                barrier()
            build_xdm()

            for hg in range(4):
                g.wait_ge(s_mm_h, s_mm.val)
                g.wait_ge(s_tr_h, s_tr.val)
                s_lg.inc(g.dma_start(out=wqs[:, :],
                    in_=wqt_h[li, hg]), 16)
                barrier()
                for tt in range(NTL):
                    for comp in range(3):
                        idx = tt * 3 + comp
                        bank = idx % 2
                        if idx >= 2:
                            te.wait_ge(s_sc_h, s_sc.val - 1)
                        for kp in range(8):
                            mm = te.matmul(
                                ps_a[:, bank * 512: bank * 512 + 256],
                                xs_slice(kp, tt * 128, (tt + 1) * 128),
                                wqs[:, kp * 768 + comp * 256:
                                    kp * 768 + (comp + 1) * 256],
                                start=(kp == 0), stop=(kp == 7))
                        s_mm.inc(mm)
                        sc.wait_ge(s_mm_h, s_mm.val)
                        dst = (qL, kL, vL)[comp]
                        s_sc.inc(sc.activation(
                            out=dst[:, tt * 256:(tt + 1) * 256],
                            in_=ps_a[:, bank * 512: bank * 512 + 256],
                            func=AF.Identity, bias=0.0, scale=1.0))
                barrier()
                if dbg == "xdm" and li == 0 and hg == 0:
                    sync.wait_ge(s_st_h, s_st.val)
                    s_st.inc(sync.dma_start(out=dbg_h[:, 0:13824],
                        in_=xdm[:, 0:13824]), 16)
                    barrier()
                if dbg == "qkv" and li == 0 and hg == 0:
                    for bi, buf in enumerate((qL, kL, vL)):
                        sync.wait_ge(s_st_h, s_st.val)
                        s_st.inc(sync.dma_start(out=dbg_h[:, bi * 4608:(bi + 1) * 4608],
                            in_=buf[:, :]), 16)
                    barrier()
                for (srcL, tb) in ((qL, 0), (kL, 2)):
                    for tt in range(NTL):
                        s0 = srcL[:, tt * 256:(tt + 1) * 256]
                        s0v = s0.rearrange("p (h c) -> p h c", c=DH)
                        vwait()
                        s_ve.inc(v.tensor_tensor(out=psc[:, 0:256], in0=s0,
                                                 in1=s0, op=ALU.mult))
                        vwait()
                        s_ve.inc(v.tensor_reduce(
                            out=pcol[:, 0:4],
                            in_=psc[:, 0:256].rearrange("p (h c) -> p h c",
                                                        c=DH),
                            axis=AX.X, op=ALU.add))
                        sc.wait_ge(s_ve_h, s_ve.val)
                        s_sc.inc(sc.activation(out=pcol[:, 4:8],
                                               in_=pcol[:, 0:4],
                                               func=AF.Sqrt, bias=0.0,
                                               scale=1.0))
                        v.wait_ge(s_sc_h, s_sc.val)
                        s_ve.inc(v.tensor_scalar_max(out=pcol[:, 4:8],
                                                     in0=pcol[:, 4:8],
                                                     scalar1=1e-12))
                        vwait()
                        s_ve.inc(v.reciprocal(out=pcol[:, 8:12],
                                              in_=pcol[:, 4:8]))
                        vwait()
                        s_ve.inc(v.tensor_tensor(
                            out=psc[:, 0:256].rearrange("p (h c) -> p h c",
                                                        c=DH),
                            in0=s0v,
                            in1=pcol[:, 8:12].unsqueeze(2)
                                .broadcast_to([128, 4, DH]),
                            op=ALU.mult))
                        ca = tabc[:, (tb * NTL + tt) * DH:
                                  (tb * NTL + tt + 1) * DH]
                        cb = tabc[:, ((tb + 1) * NTL + tt) * DH:
                                  ((tb + 1) * NTL + tt + 1) * DH]
                        vwait()
                        s_ve.inc(v.tensor_tensor(
                            out=psc[:, 256:512].rearrange("p (h c) -> p h c",
                                                          c=DH),
                            in0=psc[:, 0:256].rearrange("p (h c) -> p h c",
                                                        c=DH),
                            in1=ca.unsqueeze(1).broadcast_to([128, 4, DH]),
                            op=ALU.mult))
                        pb = psc[:, 0:256]
                        rot = bass.AP(
                            pb.tensor, pb.offset + 32,
                            [[pb.ap[0][0], 128], [DH, 4], [-32, 2], [1, 32]])
                        vwait()
                        s_ve.inc(v.tensor_tensor(
                            out=psc[:, 512:768].rearrange("p (h c) -> p h c",
                                                          c=DH),
                            in0=rot,
                            in1=cb.unsqueeze(1).broadcast_to([128, 4, DH]),
                            op=ALU.mult))
                        vwait()
                        s_ve.inc(v.tensor_tensor(
                            out=s0, in0=psc[:, 256:512],
                            in1=psc[:, 512:768], op=ALU.add))
                barrier()
                if dbg == "prep" and li == 0 and hg == 0:
                    for bi, buf in enumerate((qL, kL, vL)):
                        sync.wait_ge(s_st_h, s_st.val)
                        s_st.inc(sync.dma_start(out=dbg_h[:, bi * 4608:(bi + 1) * 4608],
                            in_=buf[:, :]), 16)
                    barrier()
                for (srcL, dstD) in ((qL, qdm), (kL, kdm)):
                    for tt in range(NTL):
                        for hp in range(2):
                            s_tr.inc(sync.dma_start(out=dstD[:, hp * LT + tt * 128:
                                        hp * LT + (tt + 1) * 128],
                                in_=srcL[:, tt * 256 + hp * 128:
                                         tt * 256 + (hp + 1) * 128],
                                transpose=True), 16)
                    sync.wait_ge(s_tr_h, s_tr.val)
                barrier()
                tp_mm = {}
                atd_ev = {}
                av_mm = {}
                ev_mark = None
                ig = 0
                for hp in range(2):
                  for hh in range(2):
                    tiles = [(w, tqt) for w in range(1, 9) for tqt in range(2)]
                    for (w, tqt) in tiles:
                        i = ig
                        p4 = i % 4
                        pq = (ps_a, ps_b)[p4 // 2][:, (p4 % 2) * 512:
                                                   (p4 % 2) * 512 + 512]
                        ab4 = arena[:, 19968 + p4 * 512:19968 + (p4 + 1) * 512]
                        at4 = arena[:, 22016 + p4 * 512:22016 + (p4 + 1) * 512]
                        nm4 = pcol[:, p4:p4 + 1]
                        dn4 = pcol[:, 4 + p4:5 + p4]
                        rd4 = pcol[:, 8 + p4:9 + p4]
                        if i >= 4:
                            te.wait_ge(s_sc_h, atd_ev[i - 4])
                        q0 = hp * LT + (w * 256 + tqt * 128)
                        k0 = hp * LT + (w - 1) * 256
                        te.matmul(
                            pq,
                            qdm[hh * 64:(hh + 1) * 64, q0:q0 + 128],
                            kdm[hh * 64:(hh + 1) * 64, k0:k0 + 512],
                            start=True, stop=False, skip_group_check=True)
                        mvar = 0 if w == 1 else 1
                        mm = te.matmul(
                            pq,
                            id128[:, :],
                            masks[:, (mvar * 2 + tqt) * 512:
                                  (mvar * 2 + tqt + 1) * 512],
                            start=False, stop=True, skip_group_check=True)
                        s_mm.inc(mm)
                        v.wait_ge(s_mm_h, s_mm.val)
                        vwait()
                        s_ve.inc(v.tensor_reduce(
                            out=nm4, in_=pq,
                            axis=AX.X, op=ALU.max, negate=True))
                        sc.wait_ge(s_ve_h, s_ve.val)
                        if i >= 4:
                            sc.wait_ge(s_mm_h, tp_mm[i - 4])
                        s_sc.inc(sc.activation(
                            out=ab4, in_=pq,
                            func=AF.Exp, bias=nm4,
                            scale=1.0, accum_out=dn4))
                        v.wait_ge(s_sc_h, s_sc.val)
                        vwait()
                        s_ve.inc(v.reciprocal(out=rd4, in_=dn4))
                        vwait()
                        s_ve.inc(v.tensor_scalar_mul(
                            out=ab4, in0=ab4, scalar1=rd4))
                        te.wait_ge(s_ve_h, s_ve.val)
                        for ci in range(4):
                            mm = te.matmul(
                                pq[:, ci * 128:(ci + 1) * 128],
                                ab4[:, ci * 128:(ci + 1) * 128],
                                id128[:, :], start=True, stop=True,
                                skip_group_check=True)
                        s_mm.inc(mm)
                        tp_mm[i] = s_mm.val
                        sc.wait_ge(s_mm_h, tp_mm[i])
                        s_sc.inc(sc.activation(
                            out=at4, in_=pq,
                            func=AF.Identity, bias=0.0, scale=1.0))
                        atd_ev[i] = s_sc.val
                        te.wait_ge(s_sc_h, atd_ev[i])
                        if i % 16 == 0 and hh == 0 and ev_mark is not None:
                            te.wait_ge(s_sc_h, ev_mark)
                        oc0 = (w - 1) * 256 + tqt * 128
                        for ci in range(4):
                            tkt = 2 * (w - 1) + ci
                            mm = te.matmul(
                                ps_av[hh * 64:(hh + 1) * 64,
                                      oc0:oc0 + 128],
                                vL[:, tkt * 256 + (hp * 2 + hh) * 64:
                                   tkt * 256 + (hp * 2 + hh) * 64 + 64],
                                at4[:, ci * 128:(ci + 1) * 128],
                                start=(ci == 0), stop=(ci == 3),
                                skip_group_check=True)
                        s_mm.inc(mm)
                        av_mm[i] = s_mm.val
                        ig += 1
                    if hh == 0:
                        continue
                    sc.wait_ge(s_mm_h, s_mm.val)
                    pr = (hg % 2) * 2 + hp
                    s_sc.inc(sc.activation(
                        out=att[:, pr * T2:(pr + 1) * T2],
                        in_=ps_av[:, :], func=AF.Identity, bias=0.0,
                        scale=1.0))
                    ev_mark = s_sc.val
                barrier()
                if hp == 1 and hg % 2 == 1:
                    load_wproj_wo(li, hg // 2)
                    barrier()
                    proj_half(with_bias=False)
                    barrier()
            if stop_after == ("a", li):
                return

            build_xdm()
            s_ld.inc(sync.dma_start(out=gcolb[:, :], in_=flng_h[li]), 16)
            s_ld.inc(sync.dma_start(out=bcolb[:, :], in_=flnb_h[li]), 16)
            s_ld.inc(sync.dma_start(out=b1c[:, :], in_=fb1_h[li]), 16)
            s_lg.inc(g.dma_start(out=brow[:, :], in_=fb2_h[li]), 16)
            barrier()
            NCK = 22
            cw_last = DIM_INNER - 21 * 128

            def w2slot(ocp):
                if ocp < 11:
                    return w2sA[:, ocp * 1024:(ocp + 1) * 1024]
                if ocp < 19:
                    return att[:, (ocp - 11) * 1024:(ocp - 10) * 1024]
                if ocp < 21:
                    return xhalo[:, (ocp - 19) * 1024:(ocp - 18) * 1024]
                return tabc[:, 0:1024]

            for ocp in range(NCK):
                s_ld.inc(sync.dma_start(out=w2slot(ocp),
                    in_=w2t_h[li, :, ocp * 1024:(ocp + 1) * 1024]), 16)
            w2_ready = s_ld.val
            for tb in range(4):
                tc0 = tb * 512
                ns = tb
                c0 = HALO + ns * 512
                for kp in range(8):
                    te.matmul(ps_a[0:1, 0:512], onesc[:, :],
                              xs_slice(kp, c0, c0 + 512),
                              start=(kp == 0), stop=(kp == 7),
                              skip_group_check=True)
                for kp in range(8):
                    sc.wait_ge(s_tr_h, s_tr.val)
                    if kp > 0:
                        sc.wait_ge(s_mm_h, s_mm.val)
                    s_sc.inc(sc.activation(out=sq_sc[:, :],
                                           in_=xs_slice(kp, c0, c0 + 512),
                                           func=AF.Square, bias=0.0,
                                           scale=1.0))
                    te.wait_ge(s_sc_h, s_sc.val)
                    mm = te.matmul(ps_a[0:1, 512:1024], onesc[:, :],
                                   sq_sc[:, :], start=(kp == 0),
                                   stop=(kp == 7), skip_group_check=True)
                    s_mm.inc(mm)
                barrier()
                s_ve.inc(v.tensor_scalar_mul(out=srow[0:1, :],
                                             in0=ps_a[0:1, 0:512],
                                             scalar1=1.0 / D))
                vwait()
                s_ve.inc(v.tensor_tensor(out=srow[32:33, :],
                                         in0=srow[0:1, :], in1=srow[0:1, :],
                                         op=ALU.mult))
                vwait()
                s_ve.inc(v.scalar_tensor_tensor(
                    out=srow[32:33, :], in0=ps_a[0:1, 512:1024],
                    scalar=1.0 / D, in1=srow[32:33, :],
                    op0=ALU.mult, op1=ALU.subtract))
                sc.wait_ge(s_ve_h, s_ve.val)
                s_sc.inc(sc.activation(out=srow[32:33, :], in_=srow[32:33, :],
                                       func=AF.Sqrt, bias=pcol[0:1, 22:23],
                                       scale=1.0))
                v.wait_ge(s_sc_h, s_sc.val)
                s_ve.inc(v.reciprocal(out=srow[32:33, :], in_=srow[32:33, :]))
                barrier()
                s_lg.inc(g.dma_start(out=rrep[:, 0:512],
                                     in_=srow[0:1, :].unsqueeze(1).broadcast_to([1, 128, 512])), 16)
                s_lg.inc(g.dma_start(out=rrep[:, 512:1024],
                                     in_=srow[32:33, :].unsqueeze(1)
                                     .broadcast_to([1, 128, 512])), 16)
                barrier()
                for kp in range(8):
                    vwait()
                    s_ve.inc(v.tensor_tensor(
                        out=psc[:, 0:512], in0=xs_slice(kp, c0, c0 + 512),
                        in1=rrep[:, 0:512], op=ALU.subtract))
                    vwait()
                    s_ve.inc(v.scalar_tensor_tensor(
                        out=psc[:, 512:1024], in0=psc[:, 0:512],
                        scalar=gcolb[:, kp:kp + 1], in1=rrep[:, 512:1024],
                        op0=ALU.mult, op1=ALU.mult))
                    vwait()
                    s_ve.inc(v.tensor_scalar_add(
                        out=xnb[:, kp * 512:(kp + 1) * 512],
                        in0=psc[:, 512:1024], scalar1=bcolb[:, kp:kp + 1]))
                barrier()
                w1mm = {}
                for ocp in range(NCK):
                    cw = 128 if ocp < 21 else cw_last
                    buf = (w1s, w1s_b)[ocp % 2]
                    sem = (s_lg, s_w1)[ocp % 2]
                    if ocp >= 2:
                        g.wait_ge(s_mm_h, w1mm[ocp - 2])
                    if ocp < 21:
                        sem.inc(g.dma_start(out=buf[:, 0:2048],
                            in_=w1t_h[li, :, ocp * 2048:(ocp + 1) * 2048]), 16)
                    else:
                        sem.inc(g.dma_start(out=buf[:, 0:8 * cw],
                            in_=w1t_h[li, :, 43008:43008 + 8 * cw]), 16)
                        sem.inc(g.dma_start(out=buf[:, 1024:1024 + 8 * cw],
                            in_=w1t_h[li, :, 43344:43344 + 8 * cw]), 16)
                    te.wait_ge(sem.h, sem.val)
                    if tb + ocp > 0:
                        te.wait_ge(s_ve_h, s_ve.val)
                    for which in range(2):
                        for kp in range(8):
                            mm = te.matmul(
                                ps_a[0:cw, which * 512:(which + 1) * 512],
                                buf[:, which * 8 * 128 + kp * cw:
                                    which * 8 * 128 + (kp + 1) * cw],
                                xnb[:, kp * 512:(kp + 1) * 512],
                                start=(kp == 0), stop=(kp == 7))
                        s_mm.inc(mm)
                    w1mm[ocp] = s_mm.val
                    sc.wait_ge(s_mm_h, s_mm.val)
                    s_sc.inc(sc.activation(
                        out=ffd[0:cw, ocp * 512:(ocp + 1) * 512],
                        in_=ps_a[0:cw, 0:512], func=AF.Identity,
                        bias=b1c[0:cw, ocp:ocp + 1], scale=1.0))
                    s_sc.inc(sc.activation(
                        out=sq_sc[0:cw, :],
                        in_=ps_a[0:cw, 512:1024], func=AF.Gelu,
                        bias=b1c[0:cw, 22 + ocp:23 + ocp], scale=1.0))
                    v.wait_ge(s_sc_h, s_sc.val)
                    vwait()
                    s_ve.inc(v.tensor_tensor(
                        out=ffd[0:cw, ocp * 512:(ocp + 1) * 512],
                        in0=ffd[0:cw, ocp * 512:(ocp + 1) * 512],
                        in1=sq_sc[0:cw, :], op=ALU.mult))
                te.wait_ge(s_ld_h, w2_ready)
                te.wait_ge(s_ve_h, s_ve.val)
                ve_evac = {}
                for ttin in range(4):
                    tt = tb * 4 + ttin
                    psd = (ps_b, ps_a)[ttin % 2]
                    if ttin >= 2:
                        te.wait_ge(s_ve_h, ve_evac[ttin - 2])
                    for oc in range(2):
                        for ocp in range(NCK):
                            cw = 128 if ocp < 21 else cw_last
                            te.matmul(
                                psd[:, oc * 512:(oc + 1) * 512],
                                ffd[0:cw, ocp * 512 + ttin * 128:
                                    ocp * 512 + ttin * 128 + 128],
                                w2slot(ocp)[0:cw, oc * 512:(oc + 1) * 512],
                                start=(ocp == 0), stop=False,
                                skip_group_check=True)
                        mm = te.matmul(psd[:, oc * 512:(oc + 1) * 512],
                                       ones1[:, :],
                                       brow[:, oc * 512:(oc + 1) * 512],
                                       start=False, stop=True,
                                       skip_group_check=True)
                        s_mm.inc(mm)
                    v.wait_ge(s_mm_h, s_mm.val)
                    for oc in range(2):
                        vwait()
                        s_ve.inc(v.tensor_tensor(
                            out=xown[:, tt * D + oc * 512:
                                     tt * D + (oc + 1) * 512],
                            in0=xown[:, tt * D + oc * 512:
                                     tt * D + (oc + 1) * 512],
                            in1=psd[:, oc * 512:(oc + 1) * 512],
                            op=ALU.add))
                    ve_evac[ttin] = s_ve.val
                barrier()

        done = stop_after == "bridge"
        if not done:
            for li in range(nlayer):
                halo_exchange()
                emit_layer(li)
                if stop_after in (("f", li), ("a", li)):
                    done = True
                    break

        if not done:
          for tt in range(NTO):
            sync.wait_ge(s_ve_h, s_ve.val)
            s_ld.inc(sync.dma_start(out=psc[:, 0:512],
                in_=xmid_st.ap()[tt * 128:(tt + 1) * 128, 0:512]), 16)
            s_ld.inc(sync.dma_start(out=psc[:, 512:1024],
                in_=xmid_st.ap()[tt * 128:(tt + 1) * 128, 512:1024]), 16)
            v.wait_ge(s_ld_h, s_ld.val)
            vwait()
            s_ve.inc(v.tensor_tensor(
                out=xown[:, tt * D: (tt + 1) * D],
                in0=xown[:, tt * D: (tt + 1) * D],
                in1=psc[:, 0:1024], op=ALU.add))
        barrier()
        if not done:
            ln_inplace()
        barrier()
        for tt in range(NTO):
            s_st.inc(sync.dma_start(out=out_h[tt * 128:(tt + 1) * 128, :],
                                    in_=xown[:, tt * D:(tt + 1) * D]), 16)
        sync.wait_ge(s_st_h, s_st.val)

    es_rest.close()
    es.close()
    return nc



def mixer_inputs_for_core(inputs, b, gidx, n_heads=HG, Lm=L):
    x = np.ascontiguousarray(inputs["L0_attn_x"][b, :Lm]).astype(np.float32)
    Wq = inputs["L0_attn_Wqkv_w"]
    bq = inputs["L0_attn_Wqkv_b"]
    h0 = gidx * n_heads
    rows, brows = [], []
    for c in range(3):
        sl = slice(c * D + h0 * DH, c * D + (h0 + n_heads) * DH)
        rows.append(Wq[sl])
        brows.append(bq[sl])
    Wslice = np.concatenate(rows, axis=0)
    bslice = np.concatenate(brows, axis=0)
    lam = inputs["L0_attn_Lambda_raw"][h0:h0 + n_heads]
    gam = inputs["L0_attn_Gamma"][h0:h0 + n_heads]
    wst = inputs["L0_attn_W_static"][h0:h0 + n_heads]
    lam_c = np.ascontiguousarray(lam.reshape(n_heads, NJT, 128).transpose(0, 2, 1))
    gam_c = np.ascontiguousarray(gam.reshape(n_heads, NJT, 128).transpose(0, 2, 1))
    selw = np.zeros((128, 128), dtype=ml_dtypes.bfloat16)
    selw[np.arange(128), 64 + np.arange(128) // 64] = 1
    nmt = 3 * n_heads * DH // 128
    return {
        "mx_x": x.astype(ml_dtypes.bfloat16),
        "mx_wqkvT": np.ascontiguousarray(Wslice.T).astype(ml_dtypes.bfloat16),
        "mx_bqkv": np.ascontiguousarray(bslice.reshape(nmt, 128).T).astype(np.float32),
        "mx_lam": lam_c.astype(np.float32),
        "mx_gam": gam_c.astype(np.float32),
        "mx_wstT": np.ascontiguousarray(wst.transpose(0, 2, 1)).astype(np.float32),
        "mx_selw": selw,
    }


def _rot_tables(g):
    n = L
    t_loc = np.arange(g * T2 - HALO, g * T2 + T2)
    t = np.clip(t_loc, 0, n - 1).astype(np.float64)
    half = np.arange(0, DH, 2, dtype=np.float64)
    inv_freq = 1.0 / 10000.0 ** (half / DH)
    fr = t[:, None] * inv_freq[None, :]
    f = np.concatenate([fr, fr], axis=1)
    base = (half + 0.4 * DH) / (1.4 * DH)
    power = (t - n // 2) / (WINDOW // 2)
    scl = base[None, :] ** power[:, None]
    scl = np.concatenate([scl, scl], axis=1)
    sgn = np.where(np.arange(DH) < DH // 2, -1.0, 1.0)[None, :]
    ca_q = np.cos(f) * scl
    cb_q = sgn * np.sin(f) * scl
    ca_k = np.cos(f) / scl
    cb_k = sgn * np.sin(f) / scl
    return np.stack([ca_q, cb_q, ca_k, cb_k]).astype(np.float32)


def _masks_for_core(g):
    out = np.zeros((2, 2, 128, 512), dtype=np.float32)
    jj = np.arange(512)[None, :]
    for var in range(2):
        for tqt in range(2):
            tau = (tqt * 128 + np.arange(128))[:, None]
            valid = (jj >= tau) & (jj <= tau + 256)
            if var == 0 and g == 0:
                valid = valid & (jj >= 256)
            out[var, tqt] = np.where(valid, 0.0, -1e9)
    return out.astype(ml_dtypes.bfloat16)


def shared_rest_weights(inputs):
    BF = ml_dtypes.bfloat16
    W = inputs["L0_ffn_Wqkv"].astype(np.float32)
    fwqkv = (W.reshape(NLAYER, D, 3, 4, 4 * DH).transpose(0, 1, 3, 2, 4)
             .reshape(NLAYER, D, 4, 768))
    wqt = np.ascontiguousarray(
        fwqkv.reshape(NLAYER, 8, 128, 4, 768).transpose(0, 3, 2, 1, 4)
        .reshape(NLAYER, 4, 128, 8 * 768)).astype(BF)
    W1 = inputs["L0_ffn_W1"].astype(np.float32)
    w1t = np.zeros((NLAYER, 128, 43680), dtype=BF)
    off = 0
    for ocp in range(22):
        cw = 128 if ocp < 21 else DIM_INNER - 21 * 128
        for which in range(2):
            base = which * DIM_INNER
            blk = (W1[:, :, base + ocp * 128: base + ocp * 128 + cw]
                   .reshape(NLAYER, 8, 128, cw).transpose(0, 2, 1, 3)
                   .reshape(NLAYER, 128, 8 * cw))
            w1t[:, :, off:off + 8 * cw] = blk.astype(BF)
            off += 8 * cw
    W2 = inputs["L0_ffn_W2"].astype(np.float32)
    w2t = np.zeros((NLAYER, 128, 22528), dtype=BF)
    for ocp in range(22):
        cw = 128 if ocp < 21 else DIM_INNER - 21 * 128
        w2t[:, 0:cw, ocp * 1024:(ocp + 1) * 1024] = \
            W2[:, ocp * 128: ocp * 128 + cw, :].astype(BF)
    b1 = inputs["L0_ffn_b1"].astype(np.float32)
    b1c = np.zeros((NLAYER, 128, 44), dtype=np.float32)
    for ocp in range(22):
        cw = 128 if ocp < 21 else DIM_INNER - 21 * 128
        b1c[:, :cw, ocp] = b1[:, ocp * 128: ocp * 128 + cw]
        b1c[:, :cw, 22 + ocp] = b1[:, DIM_INNER + ocp * 128:
                                   DIM_INNER + ocp * 128 + cw]
    lng = np.ascontiguousarray(
        inputs["L0_ffn_ln_g"].astype(np.float32).reshape(NLAYER, 8, 128)
        .transpose(0, 2, 1))
    lnb = np.ascontiguousarray(
        inputs["L0_ffn_ln_b"].astype(np.float32).reshape(NLAYER, 8, 128)
        .transpose(0, 2, 1))
    return {
        "r_outwT": np.ascontiguousarray(
            inputs["L0_attn_out_w"].astype(np.float32).T).astype(BF),
        "r_outb": inputs["L0_attn_out_b"].astype(np.float32).reshape(1, D),
        "r_wqt": wqt,
        "r_fwo": np.ascontiguousarray(
            inputs["L0_ffn_Wo"].astype(np.float32)).astype(BF),
        "r_w1t": w1t,
        "r_w2t": w2t,
        "r_fb1": b1c,
        "r_fb2": inputs["L0_ffn_b2"].astype(np.float32).reshape(NLAYER, 1, D),
        "r_flng": lng,
        "r_flnb": lnb,
        "r_fqs": inputs["L0_ffn_q_scale"].astype(np.float32).reshape(NLAYER, 1, DH),
        "r_fks": inputs["L0_ffn_k_scale"].astype(np.float32).reshape(NLAYER, 1, DH),
        "r_id128": np.eye(128, dtype=ml_dtypes.bfloat16),
        "r_ones": np.ones((1, 128), dtype=ml_dtypes.bfloat16),
    }


def rest_inputs_for_core(inputs, b, g):
    BF = ml_dtypes.bfloat16
    tabs = _rot_tables(g)
    tabt = np.ascontiguousarray(
        tabs.reshape(4, NTL, 128, DH).transpose(0, 2, 1, 3)
        .reshape(4, 128, NTL * DH)).astype(BF)
    hg2 = np.zeros((128, 2), dtype=np.float32)
    hg2[:, 0] = float(g)
    hg2[:, 1] = 1.0 - float(g)
    return {
        "r_x": np.ascontiguousarray(
            inputs["L0_attn_x"][b, g * T2:(g + 1) * T2]).astype(np.float32),
        "r_tabt": tabt,
        "r_mask": _masks_for_core(g),
        "r_hgate": hg2,
    }


def kernel(**inputs):
    nc = build_fused(nlayer=NLAYER)
    shared = shared_rest_weights(inputs)
    in_maps = []
    for c in range(8):
        b, g = c // 2, c % 2
        m = mixer_inputs_for_core(inputs, b, g)
        m.update(shared)
        m.update(rest_inputs_for_core(inputs, b, g))
        in_maps.append(m)
    res = run_bass_kernel_spmd(nc, in_maps, core_ids=list(range(8)))

    out = np.empty((B, L, D), dtype=np.float32)
    for c in range(8):
        b, g = c // 2, c % 2
        out[b, g * T2:(g + 1) * T2] = res.results[c]["r_out"]
    return out
